# version 01
# baseline (speedup 1.0000x reference)
import contextlib
import numpy as np

import concourse.bass as bass
import concourse.mybir as mybir
from concourse.bass_utils import run_bass_kernel_spmd

F32 = mybir.dt.float32
BF16 = mybir.dt.bfloat16
AF = mybir.ActivationFunctionType
OP = mybir.AluOpType
AX = mybir.AxisListType

N_CORES = 8
B, T, D = 2, 8192, 1024
TPC = 2048
HALO = 64
WIN = TPC + HALO
KD = D // 128
EPS_LN = 1e-5
EPS_RMS = 1e-6

D_INNER = 2048
D_STATE = 16
DT_RANK = 64
D_CONV = 4
DSL = 512
N_EXP = 8
H_MOE = 2048


ENGINES = ("sync", "vector", "scalar", "tensor", "gpsimd")


class Sched:

    def __init__(self):
        self.q = {e: [] for e in ENGINES}
        self.n = {}
        self.sem_names = []

    def sem(self, name):
        if name not in self.n:
            self.n[name] = 0
            self.sem_names.append(name)
        return name

    def add(self, eng, emit, waits=(), inc=None, inc_amt=1):
        waits = tuple((sm, v) for sm, v in waits if v > 0)
        if inc is not None:
            self.sem(inc)
            self.n[inc] += inc_amt
        self.q[eng].append((emit, waits, inc, inc_amt))
        return self.n[inc] if inc is not None else None

    def v(self, sem):
        return self.n.get(sem, 0)

    def build(self, nc, final_waits):
        with contextlib.ExitStack() as st:
            handles = {name: st.enter_context(nc.semaphore(name)) for name in self.sem_names}
            block = st.enter_context(nc.Block())

            def emit_engine(eng_obj, eng_name):
                for emit, waits, inc, inc_amt in self.q[eng_name]:
                    for sm, v in waits:
                        eng_obj.wait_ge(handles[sm], v)
                    inst = emit(eng_obj)
                    if inc is not None:
                        inst.then_inc(handles[inc], inc_amt)
                if eng_name == "gpsimd":
                    for sm, v in final_waits:
                        eng_obj.wait_ge(handles[sm], v)

            @block.sync
            def _(e):
                emit_engine(e, "sync")

            @block.vector
            def _(e):
                emit_engine(e, "vector")

            @block.scalar
            def _(e):
                emit_engine(e, "scalar")

            @block.tensor
            def _(e):
                emit_engine(e, "tensor")

            @block.gpsimd
            def _(e):
                emit_engine(e, "gpsimd")


class Alloc:
    def __init__(self, nc):
        self.nc = nc
        self.i = 0

    def sb(self, shape, dtype, name=None):
        self.i += 1
        return self.nc.alloc_sbuf_tensor(name or f"sb{self.i}", shape, dtype)

    def ps(self, shape, dtype=F32, name=None):
        self.i += 1
        return self.nc.alloc_psum_tensor(name or f"ps{self.i}", shape, dtype)


def t_chunks(total, step=512):
    out = []
    c0 = 0
    while c0 < total:
        w = min(step, total - c0)
        out.append((c0, w))
        c0 += w
    return out



def load_cast_weight(s, al, src_ap, kd, width, tiles=None, stage=None, dst_dtype=BF16,
                     extra_war=()):
    if stage is None:
        stage = al.sb([128, width], F32)
    if tiles is None:
        tiles = [al.sb([128, width], dst_dtype) for _ in range(kd)]
    for k in range(kd):
        war = [("V", s.v("V"))] + (list(extra_war) if k == 0 else [])
        s.add("sync", (lambda e, k=k, stage=stage, ap=src_ap, w=width:
                       e.dma_start(out=stage[:, :w], in_=ap[k * 128:(k + 1) * 128, :])),
              waits=war, inc="LD", inc_amt=16)
        s.add("vector", (lambda v, k=k, tiles=tiles, stage=stage, w=width:
                         v.tensor_copy(tiles[k][:, :w], stage[:, :w])),
              waits=[("LD", s.v("LD"))], inc="V")
    return tiles, stage


def load_bias_cols(s, al, bias_ap, kd):
    cols = []
    for k in range(kd):
        col = al.sb([128, 1], F32)
        s.add("sync", (lambda e, k=k, col=col, ap=bias_ap:
                       e.dma_start(out=col[:, :], in_=ap[k * 128:(k + 1) * 128].unsqueeze(1))),
              inc="LD", inc_amt=16)
        cols.append(col)
    return cols


def ln_bufs(al):
    return dict(
        sq=al.sb([128, 512], F32),
        stat=al.ps([1, 512], F32),
        mu_row=al.sb([1, 512], F32),
        msq_row=al.sb([1, 512], F32),
        rstd_row=al.sb([1, 512], F32),
        tmp_row=al.sb([1, 512], F32),
        bc_ps=al.ps([128, 512], F32),
        rstd_b=al.sb([128, 512], F32),
        mur_b=al.sb([128, 512], F32),
    )


def layernorm_fm(s, al, x_tiles, width, out_tiles, ones_col, ones_row,
                 gamma_col=None, beta_col=None, rms=False, eps=EPS_LN, nd=None,
                 bufs=None):
    kd = len(x_tiles)
    if nd is None:
        nd = kd * 128
    if bufs is None:
        bufs = ln_bufs(al)
    sq, stat, mu_row, msq_row = bufs["sq"], bufs["stat"], bufs["mu_row"], bufs["msq_row"]
    rstd_row, tmp_row, bc_ps = bufs["rstd_row"], bufs["tmp_row"], bufs["bc_ps"]
    rstd_b, mur_b = bufs["rstd_b"], bufs["mur_b"]

    for c0, w in t_chunks(width):
        if not rms:
            for k in range(kd):
                s.add("tensor", (lambda e, k=k, c0=c0, w=w:
                                 e.matmul(stat[:, :w], ones_col[:, :], x_tiles[k][:, c0:c0 + w],
                                          start=(k == 0), stop=(k == kd - 1))),
                      waits=[("V", s.v("V")), ("S", s.v("S"))] if k == 0 else [],
                      inc="PE" if k == kd - 1 else None)
            s.add("vector", (lambda v, w=w:
                             v.tensor_scalar_mul(mu_row[:, :w], stat[:, :w], 1.0 / nd)),
                  waits=[("PE", s.v("PE"))], inc="V")
        for k in range(kd):
            s.add("scalar", (lambda e, k=k, c0=c0, w=w:
                             e.activation(sq[:, :w], x_tiles[k][:, c0:c0 + w], AF.Square)),
                  waits=[("PE", s.v("PE"))] if k > 0 else [("V", s.v("V"))],
                  inc="S")
            s.add("tensor", (lambda e, k=k, w=w:
                             e.matmul(stat[:, :w], ones_col[:, :], sq[:, :w],
                                      start=(k == 0), stop=(k == kd - 1))),
                  waits=([("S", s.v("S")), ("V", s.v("V"))] if k == 0 else [("S", s.v("S"))]),
                  inc="PE")
        s.add("vector", (lambda v, w=w:
                         v.tensor_scalar_mul(msq_row[:, :w], stat[:, :w], 1.0 / nd)),
              waits=[("PE", s.v("PE"))], inc="V")
        if not rms:
            s.add("vector", (lambda v, w=w: v.tensor_mul(tmp_row[:, :w], mu_row[:, :w], mu_row[:, :w])),
                  inc="V")
            s.add("vector", (lambda v, w=w: v.tensor_sub(msq_row[:, :w], msq_row[:, :w], tmp_row[:, :w])),
                  inc="V")
        s.add("vector", (lambda v, w=w: v.tensor_scalar_add(msq_row[:, :w], msq_row[:, :w],
                                                            float(eps))),
              inc="V")
        s.add("scalar", (lambda e, w=w: e.activation(tmp_row[:, :w], msq_row[:, :w], AF.Sqrt)),
              waits=[("V", s.v("V"))], inc="S")
        s.add("vector", (lambda v, w=w: v.reciprocal(rstd_row[:, :w], tmp_row[:, :w])),
              waits=[("S", s.v("S"))], inc="V")
        if not rms:
            s.add("vector", (lambda v, w=w: v.tensor_mul(tmp_row[:, :w], mu_row[:, :w], rstd_row[:, :w])),
                  inc="V")
        s.add("tensor", (lambda e, w=w:
                         e.matmul(bc_ps[:, :w], ones_row[:, :], rstd_row[0:1, :w],
                                  start=True, stop=True)),
              waits=[("V", s.v("V")), ("S", s.v("S"))], inc="PE")
        s.add("scalar", (lambda e, w=w: e.copy(rstd_b[:, :w], bc_ps[:, :w])),
              waits=[("PE", s.v("PE"))], inc="S")
        if not rms:
            s.add("tensor", (lambda e, w=w:
                             e.matmul(bc_ps[:, :w], ones_row[:, :], tmp_row[0:1, :w],
                                      start=True, stop=True)),
                  waits=[("S", s.v("S"))], inc="PE")
            s.add("scalar", (lambda e, w=w: e.copy(mur_b[:, :w], bc_ps[:, :w])),
                  waits=[("PE", s.v("PE"))], inc="S")
        sden = [("S", s.v("S"))]
        for k in range(kd):
            if rms:
                if gamma_col is not None:
                    s.add("vector", (lambda v, k=k, c0=c0, w=w:
                                     v.tensor_mul(sq[:, :w], x_tiles[k][:, c0:c0 + w], rstd_b[:, :w])),
                          waits=sden if k == 0 else [], inc="V")
                    s.add("vector", (lambda v, k=k, c0=c0, w=w:
                                     v.tensor_scalar(out_tiles[k][:, c0:c0 + w], sq[:, :w],
                                                     gamma_col[k][:, :], None, OP.mult)),
                          inc="V")
                else:
                    s.add("vector", (lambda v, k=k, c0=c0, w=w:
                                     v.tensor_mul(out_tiles[k][:, c0:c0 + w],
                                                  x_tiles[k][:, c0:c0 + w], rstd_b[:, :w])),
                          waits=sden if k == 0 else [], inc="V")
            else:
                s.add("vector", (lambda v, k=k, c0=c0, w=w:
                                 v.tensor_mul(sq[:, :w], x_tiles[k][:, c0:c0 + w], rstd_b[:, :w])),
                      waits=sden if k == 0 else [], inc="V")
                if gamma_col is None:
                    s.add("vector", (lambda v, k=k, c0=c0, w=w:
                                     v.tensor_sub(out_tiles[k][:, c0:c0 + w], sq[:, :w], mur_b[:, :w])),
                          inc="V")
                else:
                    s.add("vector", (lambda v, k=k, w=w:
                                     v.tensor_sub(sq[:, :w], sq[:, :w], mur_b[:, :w])),
                          inc="V")
                    s.add("vector", (lambda v, k=k, c0=c0, w=w:
                                     v.tensor_scalar(out_tiles[k][:, c0:c0 + w], sq[:, :w],
                                                     gamma_col[k][:, :], None, OP.mult)),
                          inc="V")
                    s.add("vector", (lambda v, k=k, c0=c0, w=w:
                                     v.tensor_scalar(out_tiles[k][:, c0:c0 + w],
                                                     out_tiles[k][:, c0:c0 + w],
                                                     beta_col[k][:, :], None, OP.add)),
                          inc="V")
    return bufs


def gemm_store(s, al, name, xn_tiles, w_bf, n_out_tiles, width, out_hbm, ps, bias_cols=None,
               act=AF.Identity, out_tile=None, m_off=0, extra_first_waits=()):
    if out_tile is None:
        out_tile = al.sb([128, 1024], F32)
    kd = len(xn_tiles)
    chunks = t_chunks(width)
    for m in range(n_out_tiles):
        for ci, (c0, w) in enumerate(chunks):
            half = ci % 2
            for k in range(kd):
                waits = []
                if k == 0:
                    waits = [("S", s.v("S"))]
                    if m == 0 and ci == 0:
                        waits += [("V", s.v("V"))] + list(extra_first_waits)
                s.add("tensor", (lambda e, m=m, k=k, c0=c0, w=w:
                                 e.matmul(ps[:, :w], w_bf[k][:, m * 128:(m + 1) * 128],
                                          xn_tiles[k][:, c0:c0 + w],
                                          start=(k == 0), stop=(k == kd - 1))),
                      waits=waits, inc="PE" if k == kd - 1 else None)
            bias = bias_cols[m_off + m][:, :] if bias_cols is not None else 0.0
            st_war = [("ST", s.v("ST") - 16)] if ci >= 2 else []
            s.add("scalar", (lambda e, half=half, c0=c0, w=w, bias=bias:
                             e.activation(out_tile[:, half * 512:half * 512 + w], ps[:, :w],
                                          act, bias=bias)),
                  waits=[("PE", s.v("PE"))] + st_war, inc="S")
            s.add("gpsimd", (lambda e, m=m, half=half, c0=c0, w=w:
                             e.dma_start(out=out_hbm[(m_off + m) * 128:(m_off + m + 1) * 128,
                                                     c0:c0 + w],
                                         in_=out_tile[:, half * 512:half * 512 + w])),
                  waits=[("S", s.v("S"))], inc="ST", inc_amt=16)
        s.add("scalar", lambda e: e.nop(), waits=[("ST", s.v("ST"))], inc="S")



def build_launchA():
    nc = bass.Bass("TRN2", target_bir_lowering=False)
    al = Alloc(nc)
    s = Sched()
    for nm in ("LD", "V", "S", "PE", "ST"):
        s.sem(nm)

    x_hbm = nc.dram_tensor("x_win", [D, WIN], F32, kind="ExternalInput").ap()
    inw_hbm = nc.dram_tensor("in_w", [D, D], F32, kind="ExternalInput").ap()
    inb_hbm = nc.dram_tensor("in_b", [D], F32, kind="ExternalInput").ap()
    w1_hbm = nc.dram_tensor("w1", [D, 4096], F32, kind="ExternalInput").ap()
    b1_hbm = nc.dram_tensor("b1", [4096], F32, kind="ExternalInput").ap()
    ones_hbm = nc.dram_tensor("ones", [128, 1], F32, kind="ExternalInput").ap()
    onesr_hbm = nc.dram_tensor("onesr", [1, 128], F32, kind="ExternalInput").ap()

    xproj_out = nc.dram_tensor("x_proj", [D, WIN], F32, kind="ExternalOutput").ap()
    g_out = nc.dram_tensor("g", [4096, WIN], F32, kind="ExternalOutput").ap()

    x_tiles = [al.sb([128, WIN], F32) for _ in range(KD)]
    for k in range(KD):
        s.add("sync", (lambda e, k=k: e.dma_start(out=x_tiles[k][:, :],
                                                  in_=x_hbm[k * 128:(k + 1) * 128, :])),
              inc="LD", inc_amt=16)
    ones_col = al.sb([128, 1], F32)
    ones_row = al.sb([1, 128], F32)
    s.add("sync", lambda e: e.dma_start(out=ones_col[:, :], in_=ones_hbm), inc="LD", inc_amt=16)
    s.add("sync", lambda e: e.dma_start(out=ones_row[:, :], in_=onesr_hbm), inc="LD", inc_amt=16)
    inb_cols = load_bias_cols(s, al, inb_hbm, KD)
    b1_cols = load_bias_cols(s, al, b1_hbm, 32)
    ld_all = s.v("LD")

    s.add("vector", lambda v: v.nop(), waits=[("LD", ld_all)], inc="V")
    s.add("scalar", lambda e: e.nop(), waits=[("LD", ld_all)], inc="S")

    xn = [al.sb([128, WIN], BF16) for _ in range(KD)]
    layernorm_fm(s, al, x_tiles, WIN, xn, ones_col, ones_row)

    inw_bf, wstage = load_cast_weight(s, al, inw_hbm, KD, D)
    ps = al.ps([128, 512], F32)
    gemm_store(s, al, "xproj", xn, inw_bf, KD, WIN, xproj_out, ps, bias_cols=inb_cols,
               act=AF.Identity)

    w1_tiles = [al.sb([128, 2048], BF16) for _ in range(KD)]
    w1_stage = al.sb([128, 2048], F32)
    for half in range(2):
        war = [("PE", s.v("PE"))] if half > 0 else []
        load_cast_weight(s, al, w1_hbm[:, half * 2048:(half + 1) * 2048], KD, 2048,
                         tiles=w1_tiles, stage=w1_stage, extra_war=war)
        gemm_store(s, al, f"g{half}", xn, w1_tiles, 16, WIN, g_out, ps, bias_cols=b1_cols,
                   act=AF.Gelu, m_off=half * 16)

    s.build(nc, [("ST", s.v("ST"))])
    return nc



def build_launchB1():
    nc = bass.Bass("TRN2", target_bir_lowering=False)
    al = Alloc(nc)
    s = Sched()
    for nm in ("LD", "V", "S", "PE", "ST"):
        s.sem(nm)

    xproj_hbm = nc.dram_tensor("x_proj", [D, WIN], F32, kind="ExternalInput").ap()
    g_hbm = nc.dram_tensor("g", [4096, WIN], F32, kind="ExternalInput").ap()
    w2_hbm = nc.dram_tensor("w2", [4096, D], F32, kind="ExternalInput").ap()
    b2_hbm = nc.dram_tensor("b2", [D], F32, kind="ExternalInput").ap()
    m_out = nc.dram_tensor("m", [D, WIN], BF16, kind="ExternalOutput").ap()

    b2_cols = load_bias_cols(s, al, b2_hbm, KD)
    xp_tiles = [al.sb([128, WIN], F32) for _ in range(KD)]
    for k in range(KD):
        s.add("sync", (lambda e, k=k: e.dma_start(out=xp_tiles[k][:, :],
                                                  in_=xproj_hbm[k * 128:(k + 1) * 128, :])),
              inc="LD", inc_amt=16)
    w2_bf, _ = load_cast_weight(s, al, w2_hbm, 32, D)
    ld_w2_done_v = s.v("V")

    g_bf = [al.sb([128, 512], BF16) for _ in range(32)]
    g_stage = al.sb([128, 512], F32)
    ps = al.ps([128, 512], F32)
    mod_tile = al.sb([128, 512], F32)
    m_stage = al.sb([128, 1024], BF16)

    for ci, (c0, w) in enumerate(t_chunks(WIN)):
        for k in range(32):
            war = [("V", s.v("V"))]
            if k == 0 and ci > 0:
                war += [("PE", s.v("PE"))]
            s.add("sync", (lambda e, k=k, c0=c0, w=w:
                           e.dma_start(out=g_stage[:, :w], in_=g_hbm[k * 128:(k + 1) * 128, c0:c0 + w])),
                  waits=war, inc="LD", inc_amt=16)
            s.add("vector", (lambda v, k=k, w=w: v.tensor_copy(g_bf[k][:, :w], g_stage[:, :w])),
                  waits=[("LD", s.v("LD"))], inc="V")
        for m in range(KD):
            for k in range(32):
                waits = []
                if k == 0:
                    waits = [("S", s.v("S")), ("V", s.v("V"))]
                s.add("tensor", (lambda e, m=m, k=k, w=w:
                                 e.matmul(ps[:, :w], w2_bf[k][:, m * 128:(m + 1) * 128],
                                          g_bf[k][:, :w], start=(k == 0), stop=(k == 31))),
                      waits=waits, inc="PE" if k == 31 else None)
            s.add("scalar", (lambda e, m=m, w=w:
                             e.activation(mod_tile[:, :w], ps[:, :w], AF.Identity,
                                          bias=b2_cols[m][:, :])),
                  waits=[("PE", s.v("PE"))], inc="S")
            half = m % 2
            st_war = [("ST", s.v("ST") - 16)] if (ci * KD + m) >= 2 else []
            s.add("vector", (lambda v, m=m, half=half, c0=c0, w=w:
                             v.tensor_mul(m_stage[:, half * 512:half * 512 + w],
                                          xp_tiles[m][:, c0:c0 + w], mod_tile[:, :w])),
                  waits=[("S", s.v("S"))] + st_war, inc="V")
            s.add("gpsimd", (lambda e, m=m, half=half, c0=c0, w=w:
                             e.dma_start(out=m_out[m * 128:(m + 1) * 128, c0:c0 + w],
                                         in_=m_stage[:, half * 512:half * 512 + w])),
                  waits=[("V", s.v("V"))], inc="ST", inc_amt=16)
            s.add("scalar", lambda e: e.nop(), waits=[("V", s.v("V"))], inc="S")

    s.build(nc, [("ST", s.v("ST"))])
    return nc



def build_launchB2():
    nc = bass.Bass("TRN2", target_bir_lowering=False)
    al = Alloc(nc)
    s = Sched()
    for nm in ("LD", "V", "S", "PE", "ST", "GP"):
        s.sem(nm)

    m_hbm = nc.dram_tensor("m", [D, WIN], BF16, kind="ExternalInput").ap()
    ker_hbm = nc.dram_tensor("ker", [D, 64], F32, kind="ExternalInput").ap()
    outw_hbm = nc.dram_tensor("out_w", [D, D], F32, kind="ExternalInput").ap()
    outb_hbm = nc.dram_tensor("out_b", [D], F32, kind="ExternalInput").ap()
    x_hbm = nc.dram_tensor("x_win", [D, WIN], F32, kind="ExternalInput").ap()
    mur_hbm = nc.dram_tensor("mu_r", [D], F32, kind="ExternalInput").ap()
    muk_hbm = nc.dram_tensor("mu_k", [D], F32, kind="ExternalInput").ap()
    mask_hbm = nc.dram_tensor("first_mask", [128, 1], F32, kind="ExternalInput").ap()
    ones_hbm = nc.dram_tensor("ones", [128, 1], F32, kind="ExternalInput").ap()
    onesr_hbm = nc.dram_tensor("onesr", [1, 128], F32, kind="ExternalInput").ap()

    x1_out = nc.dram_tensor("x1", [D, TPC], F32, kind="ExternalOutput").ap()
    xr_out = nc.dram_tensor("xr", [D, TPC], BF16, kind="ExternalOutput").ap()
    xk_out = nc.dram_tensor("xk", [D, TPC], BF16, kind="ExternalOutput").ap()

    VW = TPC + 1

    m_tiles = [al.sb([128, WIN], BF16) for _ in range(KD)]
    for k in range(KD):
        s.add("sync", (lambda e, k=k: e.dma_start(out=m_tiles[k][:, :],
                                                  in_=m_hbm[k * 128:(k + 1) * 128, :])),
              inc="LD", inc_amt=16)
    ker_tiles = [al.sb([128, 64], F32) for _ in range(KD)]
    for k in range(KD):
        s.add("sync", (lambda e, k=k: e.dma_start(out=ker_tiles[k][:, :],
                                                  in_=ker_hbm[k * 128:(k + 1) * 128, :])),
              inc="LD", inc_amt=16)
    ones_col = al.sb([128, 1], F32)
    ones_row = al.sb([1, 128], F32)
    mask_col = al.sb([128, 1], F32)
    s.add("sync", lambda e: e.dma_start(out=ones_col[:, :], in_=ones_hbm), inc="LD", inc_amt=16)
    s.add("sync", lambda e: e.dma_start(out=ones_row[:, :], in_=onesr_hbm), inc="LD", inc_amt=16)
    s.add("sync", lambda e: e.dma_start(out=mask_col[:, :], in_=mask_hbm), inc="LD", inc_amt=16)
    outb_cols = load_bias_cols(s, al, outb_hbm, KD)
    mur_cols = load_bias_cols(s, al, mur_hbm, KD)
    muk_cols = load_bias_cols(s, al, muk_hbm, KD)
    outw_bf, _ = load_cast_weight(s, al, outw_hbm, KD, D)
    ld_all = s.v("LD")
    s.add("vector", lambda v: v.nop(), waits=[("LD", ld_all)], inc="V")
    s.add("scalar", lambda e: e.nop(), waits=[("LD", ld_all)], inc="S")
    s.add("gpsimd", lambda e: e.nop(), waits=[("LD", ld_all)], inc="GP")

    NV_TAPS = 44
    y_tiles = [al.sb([128, VW], BF16) for _ in range(KD)]
    acc_v = al.sb([128, 512], F32)
    acc_g = al.sb([128, 512], F32)
    g_tmp = al.sb([128, 512], F32)
    for dk in range(KD):
        for c0, w in t_chunks(VW):
            s.add("vector", (lambda v, dk=dk, c0=c0, w=w:
                             v.tensor_scalar(acc_v[:, :w], m_tiles[dk][:, 63 + c0:63 + c0 + w],
                                             ker_tiles[dk][:, 0:1], None, OP.mult)),
                  inc="V")
            for kk in range(1, NV_TAPS):
                s.add("vector", (lambda v, dk=dk, kk=kk, c0=c0, w=w:
                                 v.scalar_tensor_tensor(acc_v[:, :w],
                                                        m_tiles[dk][:, 63 + c0 - kk:63 + c0 - kk + w],
                                                        ker_tiles[dk][:, kk:kk + 1],
                                                        acc_v[:, :w], OP.mult, OP.add)),
                      inc="V")
            s.add("gpsimd", (lambda g, dk=dk, c0=c0, w=w:
                             g.tensor_scalar(acc_g[:, :w], m_tiles[dk][:, 63 + c0 - NV_TAPS:63 + c0 - NV_TAPS + w],
                                             ker_tiles[dk][:, NV_TAPS:NV_TAPS + 1], None, OP.mult)),
                  inc="GP")
            for kk in range(NV_TAPS + 1, 64):
                s.add("gpsimd", (lambda g, dk=dk, kk=kk, c0=c0, w=w:
                                 g.tensor_scalar(g_tmp[:, :w],
                                                 m_tiles[dk][:, 63 + c0 - kk:63 + c0 - kk + w],
                                                 ker_tiles[dk][:, kk:kk + 1], None, OP.mult)),
                      inc="GP")
                s.add("gpsimd", (lambda g, w=w:
                                 g.tensor_add(acc_g[:, :w], acc_g[:, :w], g_tmp[:, :w])),
                      inc="GP")
            s.add("vector", (lambda v, dk=dk, c0=c0, w=w:
                             v.tensor_add(y_tiles[dk][:, c0:c0 + w], acc_v[:, :w], acc_g[:, :w])),
                  waits=[("GP", s.v("GP"))], inc="V")
            s.add("gpsimd", lambda g: g.nop(), waits=[("V", s.v("V"))], inc="GP")

    ps = al.ps([128, 512], F32)
    x1f = [al.sb([128, VW], F32) for _ in range(KD)]
    attn_tile = al.sb([128, 512], F32)
    x_stage = al.sb([128, VW], F32)
    for mm in range(KD):
        s.add("sync", (lambda e, mm=mm: e.dma_start(out=x_stage[:, :],
                                                    in_=x_hbm[mm * 128:(mm + 1) * 128, 63:WIN])),
              waits=[("V", s.v("V"))], inc="LD", inc_amt=16)
        ld_x = s.v("LD")
        for ci, (c0, w) in enumerate(t_chunks(VW)):
            for k in range(KD):
                waits = []
                if k == 0:
                    waits = [("S", s.v("S"))]
                    if mm == 0 and ci == 0:
                        waits += [("V", s.v("V"))]
                s.add("tensor", (lambda e, mm=mm, k=k, c0=c0, w=w:
                                 e.matmul(ps[:, :w], outw_bf[k][:, mm * 128:(mm + 1) * 128],
                                          y_tiles[k][:, c0:c0 + w], start=(k == 0), stop=(k == KD - 1))),
                      waits=waits, inc="PE" if k == KD - 1 else None)
            s.add("scalar", (lambda e, mm=mm, w=w:
                             e.activation(attn_tile[:, :w], ps[:, :w], AF.Identity,
                                          bias=outb_cols[mm][:, :])),
                  waits=[("PE", s.v("PE")), ("LD", ld_x)], inc="S")
            s.add("vector", (lambda v, mm=mm, c0=c0, w=w:
                             v.tensor_add(x1f[mm][:, c0:c0 + w], x_stage[:, c0:c0 + w],
                                          attn_tile[:, :w])),
                  waits=[("S", s.v("S"))], inc="V")
            s.add("scalar", lambda e: e.nop(), waits=[("V", s.v("V"))], inc="S")

    for k in range(KD):
        s.add("gpsimd", (lambda e, k=k: e.dma_start(out=x1_out[k * 128:(k + 1) * 128, :],
                                                    in_=x1f[k][:, 1:VW])),
              waits=[("V", s.v("V"))], inc="ST", inc_amt=16)

    xn1 = [m_tiles[k][:, 0:VW] for k in range(KD)]
    layernorm_fm(s, al, x1f, VW, xn1, ones_col, ones_row)
    for k in range(KD):
        s.add("vector", (lambda v, k=k: v.tensor_scalar(xn1[k][:, 0:1], xn1[k][:, 0:1],
                                                        mask_col[:, :], None, OP.mult)),
              inc="V")
    omr_cols = [al.sb([128, 1], F32) for _ in range(KD)]
    omk_cols = [al.sb([128, 1], F32) for _ in range(KD)]
    for k in range(KD):
        s.add("vector", (lambda v, k=k: v.tensor_scalar(omr_cols[k][:, :], mur_cols[k][:, :],
                                                        -1.0, 1.0, OP.mult, OP.add)),
              inc="V")
        s.add("vector", (lambda v, k=k: v.tensor_scalar(omk_cols[k][:, :], muk_cols[k][:, :],
                                                        -1.0, 1.0, OP.mult, OP.add)),
              inc="V")
    mix_tmp = al.sb([128, TPC], F32)
    mix_out = al.sb([128, TPC], BF16)
    for k in range(KD):
        for (mu_c, om_c, dst) in ((mur_cols[k], omr_cols[k], xr_out),
                                  (muk_cols[k], omk_cols[k], xk_out)):
            st_war = [("ST", s.v("ST") - 16)] if s.v("ST") >= KD * 16 + 16 else []
            s.add("vector", (lambda v, k=k, mu_c=mu_c:
                             v.tensor_scalar(mix_tmp[:, :], xn1[k][:, 1:VW], mu_c[:, :],
                                             None, OP.mult)),
                  waits=st_war, inc="V")
            s.add("vector", (lambda v, k=k, om_c=om_c:
                             v.scalar_tensor_tensor(mix_out[:, :], xn1[k][:, 0:TPC], om_c[:, :],
                                                    mix_tmp[:, :], OP.mult, OP.add)),
                  inc="V")
            s.add("gpsimd", (lambda e, k=k, dst=dst: e.dma_start(out=dst[k * 128:(k + 1) * 128, :],
                                                                 in_=mix_out[:, :])),
                  waits=[("V", s.v("V"))], inc="ST", inc_amt=16)
            s.add("vector", lambda v: v.nop(), waits=[("ST", s.v("ST"))], inc="V")

    s.build(nc, [("ST", s.v("ST"))])
    return nc



def build_launchC():
    nc = bass.Bass("TRN2", target_bir_lowering=False)
    al = Alloc(nc)
    s = Sched()
    for nm in ("LD", "V", "S", "PE", "ST"):
        s.sem(nm)

    xr_hbm = nc.dram_tensor("xr", [D, TPC], BF16, kind="ExternalInput").ap()
    xk_hbm = nc.dram_tensor("xk", [D, TPC], BF16, kind="ExternalInput").ap()
    x1_hbm = nc.dram_tensor("x1", [D, TPC], F32, kind="ExternalInput").ap()
    wr_hbm = nc.dram_tensor("WrT", [D, D], F32, kind="ExternalInput").ap()
    wk_hbm = nc.dram_tensor("WkT", [D, D], F32, kind="ExternalInput").ap()
    wv_hbm = nc.dram_tensor("WvT", [D, D], F32, kind="ExternalInput").ap()
    ones_hbm = nc.dram_tensor("ones", [128, 1], F32, kind="ExternalInput").ap()
    onesr_hbm = nc.dram_tensor("onesr", [1, 128], F32, kind="ExternalInput").ap()

    x2_out = nc.dram_tensor("x2", [D, TPC], F32, kind="ExternalOutput").ap()
    v0_out = nc.dram_tensor("v0", [D, TPC], BF16, kind="ExternalOutput").ap()

    ones_col = al.sb([128, 1], F32)
    ones_row = al.sb([1, 128], F32)
    s.add("sync", lambda e: e.dma_start(out=ones_col[:, :], in_=ones_hbm), inc="LD", inc_amt=16)
    s.add("sync", lambda e: e.dma_start(out=ones_row[:, :], in_=onesr_hbm), inc="LD", inc_amt=16)

    xk_tiles = [al.sb([128, TPC], BF16) for _ in range(KD)]
    xr_tiles = [al.sb([128, TPC], BF16) for _ in range(KD)]
    ka_tiles = [al.sb([128, TPC], BF16) for _ in range(KD)]
    for k in range(KD):
        s.add("sync", (lambda e, k=k: e.dma_start(out=xk_tiles[k][:, :],
                                                  in_=xk_hbm[k * 128:(k + 1) * 128, :])),
              inc="LD", inc_amt=16)
        s.add("sync", (lambda e, k=k: e.dma_start(out=xr_tiles[k][:, :],
                                                  in_=xr_hbm[k * 128:(k + 1) * 128, :])),
              inc="LD", inc_amt=16)
    w_bf = [al.sb([128, D], BF16) for _ in range(KD)]
    w_stage = al.sb([128, D], F32)
    ps = al.ps([128, 512], F32)
    ld_in = s.v("LD")
    s.add("vector", lambda v: v.nop(), waits=[("LD", ld_in)], inc="V")

    def gemm_to_tiles(src_tiles, out_tiles, act, post_square=False):
        for mm in range(KD):
            for ci, (c0, w) in enumerate(t_chunks(TPC)):
                for k in range(KD):
                    waits = []
                    if k == 0:
                        waits = [("S", s.v("S"))]
                        if mm == 0 and ci == 0:
                            waits = [("V", s.v("V")), ("S", s.v("S"))]
                    s.add("tensor", (lambda e, mm=mm, k=k, c0=c0, w=w:
                                     e.matmul(ps[:, :w], w_bf[k][:, mm * 128:(mm + 1) * 128],
                                              src_tiles[k][:, c0:c0 + w],
                                              start=(k == 0), stop=(k == KD - 1))),
                          waits=waits, inc="PE" if k == KD - 1 else None)
                s.add("scalar", (lambda e, mm=mm, c0=c0, w=w:
                                 e.activation(out_tiles[mm][:, c0:c0 + w], ps[:, :w], act)),
                      waits=[("PE", s.v("PE"))], inc="S")
                if post_square:
                    s.add("scalar", (lambda e, mm=mm, c0=c0, w=w:
                                     e.activation(out_tiles[mm][:, c0:c0 + w],
                                                  out_tiles[mm][:, c0:c0 + w], AF.Square)),
                          inc="S")

    load_cast_weight(s, al, wk_hbm, KD, D, tiles=w_bf, stage=w_stage)
    gemm_to_tiles(xk_tiles, ka_tiles, AF.Relu, post_square=True)
    load_cast_weight(s, al, wv_hbm, KD, D, tiles=w_bf, stage=w_stage,
                     extra_war=[("PE", s.v("PE"))])
    gemm_to_tiles(ka_tiles, xk_tiles, AF.Identity)
    load_cast_weight(s, al, wr_hbm, KD, D, tiles=w_bf, stage=w_stage,
                     extra_war=[("PE", s.v("PE"))])
    gemm_to_tiles(xr_tiles, ka_tiles, AF.Sigmoid)

    x2_tiles = [al.sb([128, TPC], F32) for _ in range(KD)]
    x1_stage = al.sb([128, TPC], F32)
    for k in range(KD):
        s.add("sync", (lambda e, k=k: e.dma_start(out=x1_stage[:, :],
                                                  in_=x1_hbm[k * 128:(k + 1) * 128, :])),
              waits=[("V", s.v("V"))], inc="LD", inc_amt=16)
        s.add("vector", (lambda v, k=k: v.tensor_mul(x2_tiles[k][:, :], ka_tiles[k][:, :],
                                                     xk_tiles[k][:, :])),
              waits=[("S", s.v("S")), ("LD", s.v("LD"))], inc="V")
        s.add("vector", (lambda v, k=k: v.tensor_add(x2_tiles[k][:, :], x2_tiles[k][:, :],
                                                     x1_stage[:, :])),
              inc="V")
        s.add("gpsimd", (lambda e, k=k: e.dma_start(out=x2_out[k * 128:(k + 1) * 128, :],
                                                    in_=x2_tiles[k][:, :])),
              waits=[("V", s.v("V"))], inc="ST", inc_amt=16)

    v0_tiles = xr_tiles
    layernorm_fm(s, al, x2_tiles, TPC, v0_tiles, ones_col, ones_row)
    for k in range(KD):
        s.add("gpsimd", (lambda e, k=k: e.dma_start(out=v0_out[k * 128:(k + 1) * 128, :],
                                                    in_=v0_tiles[k][:, :])),
              waits=[("V", s.v("V"))], inc="ST", inc_amt=16)

    s.build(nc, [("ST", s.v("ST"))])
    return nc



def build_launchD():
    nc = bass.Bass("TRN2", target_bir_lowering=False)
    al = Alloc(nc)
    s = Sched()
    for nm in ("LD", "V", "S", "PE", "ST"):
        s.sem(nm)

    v_hbm = nc.dram_tensor("v_full", [D, T], BF16, kind="ExternalInput").ap()
    inw_hbm = nc.dram_tensor("in_wT_sl", [D, 1024], F32, kind="ExternalInput").ap()
    cw_hbm = nc.dram_tensor("conv_w_sl", [DSL, D_CONV], F32, kind="ExternalInput").ap()
    cb_hbm = nc.dram_tensor("conv_b_sl", [DSL], F32, kind="ExternalInput").ap()
    xpw_hbm = nc.dram_tensor("xp_wT_sl", [DSL, 96], F32, kind="ExternalInput").ap()

    u_out = nc.dram_tensor("u", [DSL, T], BF16, kind="ExternalOutput").ap()
    z_out = nc.dram_tensor("z", [DSL, T], BF16, kind="ExternalOutput").ap()
    xdbl_out = nc.dram_tensor("xdbl_p", [96, T], F32, kind="ExternalOutput").ap()

    inw_bf, _ = load_cast_weight(s, al, inw_hbm, KD, 1024)
    cw_tiles = [al.sb([128, D_CONV], F32) for _ in range(4)]
    s.add("sync", lambda e: e.nop(), waits=[("V", s.v("V"))])
    cb_cols = load_bias_cols(s, al, cb_hbm, 4)
    for k in range(4):
        s.add("sync", (lambda e, k=k: e.dma_start(out=cw_tiles[k][:, :],
                                                  in_=cw_hbm[k * 128:(k + 1) * 128, :])),
              inc="LD", inc_amt=16)
    xpw_bf, _ = load_cast_weight(s, al, xpw_hbm, 4, 96)
    ld_const = s.v("LD")
    s.add("scalar", lambda e: e.nop(), waits=[("LD", ld_const)], inc="S")
    s.add("vector", lambda v: v.nop(), waits=[("LD", ld_const)], inc="V")

    TC = 512
    v_tiles = [al.sb([128, TC], BF16) for _ in range(KD)]
    ps = al.ps([128, 512], F32)
    ps96 = al.ps([96, 512], F32)
    xc_ext = [al.sb([128, 3 + TC], F32) for _ in range(4)]
    u_bf = [al.sb([128, TC], BF16) for _ in range(4)]
    z_bf = al.sb([128, TC], BF16)
    conv_acc = al.sb([128, TC], F32)
    xdbl_stage = al.sb([96, TC], F32)
    for k in range(4):
        s.add("vector", (lambda v, k=k: v.memset(xc_ext[k][:, 0:3], 0.0)), inc="V")

    for ci, (c0, w) in enumerate(t_chunks(T, TC)):
        for k in range(KD):
            war = []
            if k == 0:
                war = [("PE", s.v("PE")), ("V", s.v("V"))] if ci > 0 else [("V", s.v("V"))]
            s.add("sync", (lambda e, k=k, c0=c0, w=w:
                           e.dma_start(out=v_tiles[k][:, :w], in_=v_hbm[k * 128:(k + 1) * 128, c0:c0 + w])),
                  waits=war, inc="LD", inc_amt=16)
        ld_v = s.v("LD")
        for m in range(8):
            for k in range(KD):
                waits = []
                if k == 0:
                    waits = [("S", s.v("S")), ("LD", ld_v)]
                    if m == 0 and ci == 0:
                        waits += [("V", s.v("V"))]
                s.add("tensor", (lambda e, m=m, k=k, w=w:
                                 e.matmul(ps[:, :w], inw_bf[k][:, m * 128:(m + 1) * 128],
                                          v_tiles[k][:, :w], start=(k == 0), stop=(k == KD - 1))),
                      waits=waits, inc="PE" if k == KD - 1 else None)
            if m < 4:
                s.add("scalar", (lambda e, m=m, w=w:
                                 e.activation(xc_ext[m][:, 3:3 + w], ps[:, :w], AF.Identity)),
                      waits=[("PE", s.v("PE")), ("V", s.v("V"))], inc="S")
            else:
                s.add("scalar", (lambda e, w=w:
                                 e.activation(z_bf[:, :w], ps[:, :w], AF.Identity)),
                      waits=[("PE", s.v("PE")), ("ST", s.v("ST"))], inc="S")
                s.add("gpsimd", (lambda e, m=m, c0=c0, w=w:
                                 e.dma_start(out=z_out[(m - 4) * 128:(m - 3) * 128, c0:c0 + w],
                                             in_=z_bf[:, :w])),
                      waits=[("S", s.v("S"))], inc="ST", inc_amt=16)
        for dk in range(4):
            s.add("vector", (lambda v, dk=dk, w=w:
                             v.tensor_scalar(conv_acc[:, :w], xc_ext[dk][:, 0:w],
                                             cw_tiles[dk][:, 0:1], None, OP.mult)),
                  waits=[("S", s.v("S"))], inc="V")
            for kk in range(1, 4):
                s.add("vector", (lambda v, dk=dk, kk=kk, w=w:
                                 v.scalar_tensor_tensor(conv_acc[:, :w], xc_ext[dk][:, kk:kk + w],
                                                        cw_tiles[dk][:, kk:kk + 1],
                                                        conv_acc[:, :w], OP.mult, OP.add)),
                      inc="V")
            s.add("scalar", (lambda e, dk=dk, w=w:
                             e.activation(u_bf[dk][:, :w], conv_acc[:, :w], AF.Silu,
                                          bias=cb_cols[dk][:, :])),
                  waits=[("V", s.v("V")), ("PE", s.v("PE"))], inc="S")
            s.add("gpsimd", (lambda e, dk=dk, c0=c0, w=w:
                             e.dma_start(out=u_out[dk * 128:(dk + 1) * 128, c0:c0 + w],
                                         in_=u_bf[dk][:, :w])),
                  waits=[("S", s.v("S"))], inc="ST", inc_amt=16)
            s.add("vector", (lambda v, dk=dk, w=w:
                             v.tensor_copy(xc_ext[dk][:, 0:3], xc_ext[dk][:, w:w + 3])),
                  waits=[("S", s.v("S"))], inc="V")
        for k in range(4):
            s.add("tensor", (lambda e, k=k, w=w:
                             e.matmul(ps96[:, :w], xpw_bf[k][:, :], u_bf[k][:, :w],
                                      start=(k == 0), stop=(k == 3))),
                  waits=[("S", s.v("S"))] if k == 0 else [], inc="PE" if k == 3 else None)
        s.add("scalar", (lambda e, w=w: e.activation(xdbl_stage[:, :w], ps96[:, :w], AF.Identity)),
              waits=[("PE", s.v("PE")), ("ST", s.v("ST"))], inc="S")
        s.add("gpsimd", (lambda e, c0=c0, w=w:
                         e.dma_start(out=xdbl_out[:, c0:c0 + w], in_=xdbl_stage[:, :w])),
              waits=[("S", s.v("S"))], inc="ST", inc_amt=16)

    s.build(nc, [("ST", s.v("ST"))])
    return nc



def build_launchE():
    nc = bass.Bass("TRN2", target_bir_lowering=False)
    al = Alloc(nc)
    s = Sched()
    for nm in ("LD", "V", "S", "PE", "ST", "GP", "RD"):
        s.sem(nm)

    u_hbm = nc.dram_tensor("u", [DSL, T], BF16, kind="ExternalInput").ap()
    z_hbm = nc.dram_tensor("z", [DSL, T], BF16, kind="ExternalInput").ap()
    xdbl_hbm = nc.dram_tensor("xdbl_all", [4, 96, T], F32, kind="ExternalInput").ap()
    dtw_hbm = nc.dram_tensor("dt_wT_sl", [DT_RANK, DSL], F32, kind="ExternalInput").ap()
    dtb_hbm = nc.dram_tensor("dt_b_sl", [DSL], F32, kind="ExternalInput").ap()
    alog_hbm = nc.dram_tensor("A_log_sl", [DSL, D_STATE], F32, kind="ExternalInput").ap()
    dsk_hbm = nc.dram_tensor("Dskip_sl", [DSL], F32, kind="ExternalInput").ap()
    onesr_hbm = nc.dram_tensor("onesr", [1, 128], F32, kind="ExternalInput").ap()
    selbc_hbm = nc.dram_tensor("selBC", [32 * 96, 128], F32, kind="ExternalInput").ap()

    y_out = nc.dram_tensor("y", [DSL, T], BF16, kind="ExternalOutput").ap()

    ones_row = al.sb([1, 128], F32)
    s.add("sync", lambda e: e.dma_start(out=ones_row[:, :], in_=onesr_hbm), inc="LD", inc_amt=16)
    dtb_cols = load_bias_cols(s, al, dtb_hbm, 4)
    dsk_cols = load_bias_cols(s, al, dsk_hbm, 4)
    acol = []
    for k in range(4):
        t = al.sb([128, D_STATE], F32)
        s.add("sync", (lambda e, k=k, t=t: e.dma_start(out=t[:, :],
                                                       in_=alog_hbm[k * 128:(k + 1) * 128, :])),
              waits=[("V", s.v("V"))] if k > 0 else [], inc="LD", inc_amt=16)
        s.add("scalar", (lambda e, t=t: e.activation(t[:, :], t[:, :], AF.Exp)),
              waits=[("LD", s.v("LD"))], inc="S")
        s.add("vector", (lambda v, t=t: v.tensor_scalar_mul(t[:, :], t[:, :], -1.0)),
              waits=[("S", s.v("S"))], inc="V")
        acol.append(t)
    dtw_stage = al.sb([64, DSL], F32)
    dtw_bf = al.sb([64, DSL], BF16)
    s.add("sync", lambda e: e.dma_start(out=dtw_stage[:, :], in_=dtw_hbm),
          waits=[("V", s.v("V"))], inc="LD", inc_amt=16)
    s.add("vector", lambda v: v.tensor_copy(dtw_bf[:, :], dtw_stage[:, :]),
          waits=[("LD", s.v("LD"))], inc="V")
    selbc_stage = al.sb([96, 128], F32)
    for i in range(32):
        s.add("sync", (lambda e, i=i: e.dma_start(out=selbc_stage[:, :],
                                                  in_=selbc_hbm[i * 96:(i + 1) * 96, :])),
              waits=[("V", s.v("V"))], inc="LD", inc_amt=16)
        s.add("vector", (lambda v, i=i: v.tensor_copy(selbc[i][:, :], selbc_stage[:, :])),
              waits=[("LD", s.v("LD"))], inc="V")

    xdbl = al.sb([96, T], BF16)
    sp_tmp = al.sb([128, 2048], F32)
    xstage = sp_tmp
    for ci, (c0, w) in enumerate(t_chunks(T, 2048)):
        for p in range(4):
            s.add("sync", (lambda e, p=p, c0=c0, w=w:
                           e.dma_start(out=xstage[0:96, :w], in_=xdbl_hbm[p, :, c0:c0 + w])),
                  waits=[("V", s.v("V"))], inc="LD", inc_amt=16)
            if p == 0:
                s.add("vector", (lambda v, c0=c0, w=w: v.tensor_copy(xdbl[:, c0:c0 + w], xstage[0:96, :w])),
                      waits=[("LD", s.v("LD"))], inc="V")
            else:
                s.add("vector", (lambda v, c0=c0, w=w:
                                 v.tensor_add(xdbl[:, c0:c0 + w], xdbl[:, c0:c0 + w], xstage[0:96, :w])),
                      waits=[("LD", s.v("LD"))], inc="V")

    TSC = 2048
    u_tiles = [al.sb([128, TSC], BF16) for _ in range(4)]
    dt_tiles = [al.sb([128, TSC], BF16) for _ in range(4)]
    udt_tiles = [al.sb([128, TSC], BF16) for _ in range(4)]
    z_tiles = [al.sb([128, TSC], BF16) for _ in range(4)]
    a_sb = [al.sb([128, TSC], F32) for _ in range(2)]
    b_sb = al.sb([128, TSC], BF16)
    h_sb = [al.sb([128, TSC], F32) for _ in range(2)]
    zg_sb = [al.sb([128, TSC], BF16) for _ in range(2)]
    y_tiles = [al.sb([128, TSC], F32) for _ in range(4)]
    selbc = [al.sb([96, 128], BF16) for _ in range(32)]
    b_bc = [al.sb([128, TSC], BF16) for _ in range(2)]
    c_bc = [al.sb([128, TSC], BF16) for _ in range(2)]
    slab = al.sb([128, 64], F32)
    y_stage = al.sb([128, TSC], BF16)
    ps = al.ps([128, 512], F32)
    bc_ps = al.ps([128, 512], F32)
    s.add("vector", lambda v: v.memset(slab[:, :], 0.0), inc="V")

    for ci, (c0, w) in enumerate(t_chunks(T, TSC)):
        for k in range(4):
            war = ([("V", s.v("V")), ("S", s.v("S"))] if ci > 0 else [("V", s.v("V"))]) \
                if k == 0 else []
            s.add("sync", (lambda e, k=k, c0=c0, w=w:
                           e.dma_start(out=u_tiles[k][:, :w], in_=u_hbm[k * 128:(k + 1) * 128, c0:c0 + w])),
                  waits=war, inc="LD", inc_amt=16)
            s.add("sync", (lambda e, k=k, c0=c0, w=w:
                           e.dma_start(out=z_tiles[k][:, :w], in_=z_hbm[k * 128:(k + 1) * 128, c0:c0 + w])),
                  inc="LD", inc_amt=16)
        ld_uz = s.v("LD")
        for dk in range(4):
            for c1, w1 in t_chunks(w):
                s.add("tensor", (lambda e, dk=dk, c0=c0, c1=c1, w1=w1:
                                 e.matmul(ps[:, :w1], dtw_bf[:, dk * 128:(dk + 1) * 128],
                                          xdbl[0:64, c0 + c1:c0 + c1 + w1], start=True, stop=True)),
                      waits=[("S", s.v("S")), ("V", s.v("V"))] if (dk == 0 and c1 == 0) else
                            [("S", s.v("S"))],
                      inc="PE")
                s.add("scalar", (lambda e, dk=dk, c1=c1, w1=w1:
                                 e.activation(dt_tiles[dk][:, c1:c1 + w1], ps[:, :w1], AF.Identity,
                                              bias=dtb_cols[dk][:, :])),
                      waits=[("PE", s.v("PE"))], inc="S")
            s.add("scalar", (lambda e, dk=dk, w=w:
                             e.activation(sp_tmp[:, :w], dt_tiles[dk][:, :w], AF.Abs)),
                  inc="S")
            s.add("scalar", (lambda e, dk=dk, w=w:
                             e.activation(sp_tmp[:, :w], sp_tmp[:, :w], AF.Exp, scale=-1.0)),
                  inc="S")
            s.add("scalar", (lambda e, dk=dk, w=w:
                             e.activation(sp_tmp[:, :w], sp_tmp[:, :w], AF.Ln, bias=1.0)),
                  inc="S")
            s.add("vector", (lambda v, dk=dk, w=w:
                             v.tensor_scalar_max(dt_tiles[dk][:, :w], dt_tiles[dk][:, :w], 0.0)),
                  waits=[("S", s.v("S"))], inc="V")
            s.add("vector", (lambda v, dk=dk, w=w:
                             v.tensor_add(dt_tiles[dk][:, :w], dt_tiles[dk][:, :w], sp_tmp[:, :w])),
                  inc="V")
            s.add("vector", (lambda v, dk=dk, w=w:
                             v.tensor_mul(udt_tiles[dk][:, :w], dt_tiles[dk][:, :w],
                                          u_tiles[dk][:, :w])),
                  waits=[("LD", ld_uz)], inc="V")
            s.add("scalar", lambda e: e.nop(), waits=[("V", s.v("V"))], inc="S")
        for dk in range(4):
            s.add("vector", (lambda v, dk=dk, w=w: v.memset(y_tiles[dk][:, :w], 0.0)), inc="V")
        dt_ready_v = s.v("V")

        v_tts = {}
        gp_z = {}
        n_end = {}
        pend = None
        chunk_start_gp = s.v("GP")

        for n in range(D_STATE):
            par = n % 2
            if n >= 2:
                vv, gg = n_end[n - 2]
                bc_war = [("V", vv), ("GP", gg)]
            else:
                bc_war = [("V", dt_ready_v), ("GP", chunk_start_gp)]
            for c1, w1 in t_chunks(w):
                s.add("tensor", (lambda e, n=n, c0=c0, c1=c1, w1=w1:
                                 e.matmul(bc_ps[:, :w1], selbc[n][:, :],
                                          xdbl[0:96, c0 + c1:c0 + c1 + w1],
                                          start=True, stop=True)),
                      waits=[("S", s.v("S"))] + bc_war, inc="PE")
                s.add("scalar", (lambda e, par=par, c1=c1, w1=w1:
                                 e.copy(b_bc[par][:, c1:c1 + w1], bc_ps[:, :w1])),
                      waits=[("PE", s.v("PE"))] + bc_war, inc="S")
                s.add("tensor", (lambda e, n=n, c0=c0, c1=c1, w1=w1:
                                 e.matmul(bc_ps[:, :w1], selbc[16 + n][:, :],
                                          xdbl[0:96, c0 + c1:c0 + c1 + w1],
                                          start=True, stop=True)),
                      waits=[("S", s.v("S"))], inc="PE")
                s.add("scalar", (lambda e, par=par, c1=c1, w1=w1:
                                 e.copy(c_bc[par][:, c1:c1 + w1], bc_ps[:, :w1])),
                      waits=[("PE", s.v("PE"))], inc="S")
            for dk in range(4):
                it = n * 4 + dk
                apar = it % 2
                hpar = it % 2
                if pend is not None:
                    pdk, pzp, pgp = pend
                    s.add("vector", (lambda v, pdk=pdk, pzp=pzp, w=w:
                                     v.tensor_add(y_tiles[pdk][:, :w], y_tiles[pdk][:, :w],
                                                  zg_sb[pzp][:, :w])),
                          waits=[("GP", pgp)], inc="V")
                a_war = [("V", v_tts[it - 2])] if it >= 2 else [("V", dt_ready_v)]
                s.add("scalar", (lambda e, dk=dk, n=n, apar=apar, w=w:
                                 e.activation(a_sb[apar][:, :w], dt_tiles[dk][:, :w], AF.Exp,
                                              scale=acol[dk][:, n:n + 1])),
                      waits=a_war, inc="S")
                s.add("vector", (lambda v, dk=dk, par=par, w=w:
                                 v.tensor_mul(b_sb[:, :w], udt_tiles[dk][:, :w], b_bc[par][:, :w])),
                      waits=[("S", s.v("S"))], inc="V")
                tts_war = [("GP", gp_z[it - 2])] if it >= 2 else [("GP", chunk_start_gp)]
                s.add("vector", (lambda v, apar=apar, hpar=hpar, sidx=it, w=w:
                                 v.tensor_tensor_scan(h_sb[hpar][:, :w], a_sb[apar][:, :w], b_sb[:, :w],
                                                      slab[:, sidx:sidx + 1], OP.mult, OP.add)),
                      waits=tts_war, inc="V")
                v_tts[it] = s.v("V")
                s.add("vector", (lambda v, hpar=hpar, sidx=it, w=w:
                                 v.tensor_copy(slab[:, sidx:sidx + 1], h_sb[hpar][:, w - 1:w])),
                      waits=[("V", s.v("V"))], inc="V")
                s.add("gpsimd", (lambda g, par=par, hpar=hpar, zp=it % 2, w=w:
                                 g.tensor_mul(zg_sb[zp][:, :w], h_sb[hpar][:, :w], c_bc[par][:, :w])),
                      waits=[("V", s.v("V"))], inc="GP")
                gp_z[it] = s.v("GP")
                pend = (dk, it % 2, gp_z[it])
            n_end[n] = (s.v("V"), s.v("GP"))
        if pend is not None:
            pdk, pzp, pgp = pend
            s.add("vector", (lambda v, pdk=pdk, pzp=pzp, w=w:
                             v.tensor_add(y_tiles[pdk][:, :w], y_tiles[pdk][:, :w],
                                          zg_sb[pzp][:, :w])),
                  waits=[("GP", pgp)], inc="V")
            pend = None

        for dk in range(4):
            s.add("vector", (lambda v, dk=dk, w=w:
                             v.scalar_tensor_tensor(y_tiles[dk][:, :w], u_tiles[dk][:, :w],
                                                    dsk_cols[dk][:, :], y_tiles[dk][:, :w],
                                                    OP.mult, OP.add)),
                  inc="V")
            s.add("scalar", (lambda e, dk=dk, w=w:
                             e.activation(sp_tmp[:, :w], z_tiles[dk][:, :w], AF.Silu)),
                  waits=[("V", s.v("V"))], inc="S")
            st_war = [("ST", s.v("ST"))] if (ci * 4 + dk) >= 1 else []
            s.add("vector", (lambda v, dk=dk, w=w:
                             v.tensor_mul(y_stage[:, :w], y_tiles[dk][:, :w], sp_tmp[:, :w])),
                  waits=[("S", s.v("S"))] + st_war, inc="V")
            s.add("gpsimd", (lambda e, dk=dk, c0=c0, w=w:
                             e.dma_start(out=y_out[dk * 128:(dk + 1) * 128, c0:c0 + w],
                                         in_=y_stage[:, :w])),
                  waits=[("V", s.v("V"))], inc="ST", inc_amt=16)
            s.add("scalar", lambda e: e.nop(), waits=[("V", s.v("V"))], inc="S")

    s.build(nc, [("ST", s.v("ST"))])
    return nc



def build_launchF(vin_f32):
    nc = bass.Bass("TRN2", target_bir_lowering=False)
    al = Alloc(nc)
    s = Sched()
    for nm in ("LD", "V", "S", "PE", "ST"):
        s.sem(nm)

    y_hbm = nc.dram_tensor("y_b", [D_INNER, TPC], BF16, kind="ExternalInput").ap()
    ow_hbm = nc.dram_tensor("out_wT", [D_INNER, D], F32, kind="ExternalInput").ap()
    lnw_hbm = nc.dram_tensor("ln_w", [D], F32, kind="ExternalInput").ap()
    lnb_hbm = nc.dram_tensor("ln_b", [D], F32, kind="ExternalInput").ap()
    vin_hbm = nc.dram_tensor("v_in", [D, TPC], F32 if vin_f32 else BF16,
                             kind="ExternalInput").ap()
    ones_hbm = nc.dram_tensor("ones", [128, 1], F32, kind="ExternalInput").ap()
    onesr_hbm = nc.dram_tensor("onesr", [1, 128], F32, kind="ExternalInput").ap()

    vf_out = nc.dram_tensor("v_f", [D, TPC], F32, kind="ExternalOutput").ap()
    vb_out = nc.dram_tensor("v_b", [D, TPC], BF16, kind="ExternalOutput").ap()

    ones_col = al.sb([128, 1], F32)
    ones_row = al.sb([1, 128], F32)
    s.add("sync", lambda e: e.dma_start(out=ones_col[:, :], in_=ones_hbm), inc="LD", inc_amt=16)
    s.add("sync", lambda e: e.dma_start(out=ones_row[:, :], in_=onesr_hbm), inc="LD", inc_amt=16)
    g_cols = load_bias_cols(s, al, lnw_hbm, KD)
    b_cols = load_bias_cols(s, al, lnb_hbm, KD)
    ow_bf, _ = load_cast_weight(s, al, ow_hbm, 16, D)
    ld0 = s.v("LD")
    s.add("scalar", lambda e: e.nop(), waits=[("LD", ld0)], inc="S")

    y_ch = [al.sb([128, 512], BF16) for _ in range(16)]
    h_ch = [al.sb([128, 512], F32) for _ in range(KD)]
    hn_ch = [al.sb([128, 512], F32) for _ in range(KD)]
    vin_ch = [al.sb([128, 512], F32 if vin_f32 else BF16) for _ in range(KD)]
    vout_ch = [al.sb([128, 512], F32) for _ in range(KD)]
    vb_st = al.sb([128, 512], BF16)
    ps = al.ps([128, 512], F32)
    lbufs = ln_bufs(al)

    for ci, (c0, w) in enumerate(t_chunks(TPC)):
        for k in range(16):
            war = []
            if k == 0:
                war = [("PE", s.v("PE"))] if ci > 0 else [("S", s.v("S"))]
            s.add("sync", (lambda e, k=k, c0=c0, w=w:
                           e.dma_start(out=y_ch[k][:, :w], in_=y_hbm[k * 128:(k + 1) * 128, c0:c0 + w])),
                  waits=war, inc="LD", inc_amt=16)
        for k in range(KD):
            war = []
            if k == 0:
                war = [("V", s.v("V"))] if ci > 0 else [("S", s.v("S"))]
            s.add("sync", (lambda e, k=k, c0=c0, w=w:
                           e.dma_start(out=vin_ch[k][:, :w], in_=vin_hbm[k * 128:(k + 1) * 128, c0:c0 + w])),
                  waits=war, inc="LD", inc_amt=16)
        ld_ch = s.v("LD")
        for m in range(KD):
            for k in range(16):
                waits = []
                if k == 0:
                    waits = [("S", s.v("S")), ("LD", ld_ch)]
                    if m == 0 and ci == 0:
                        waits += [("V", s.v("V"))]
                s.add("tensor", (lambda e, m=m, k=k, w=w:
                                 e.matmul(ps[:, :w], ow_bf[k][:, m * 128:(m + 1) * 128],
                                          y_ch[k][:, :w], start=(k == 0), stop=(k == 15))),
                      waits=waits, inc="PE" if k == 15 else None)
            s.add("scalar", (lambda e, m=m, w=w:
                             e.activation(h_ch[m][:, :w], ps[:, :w], AF.Identity)),
                  waits=[("PE", s.v("PE")), ("V", s.v("V"))], inc="S")
        layernorm_fm(s, al, h_ch, w, hn_ch, ones_col, ones_row,
                     gamma_col=g_cols, beta_col=b_cols, bufs=lbufs)
        for k in range(KD):
            s.add("vector", (lambda v, k=k, w=w:
                             v.tensor_add(vout_ch[k][:, :w], hn_ch[k][:, :w], vin_ch[k][:, :w])),
                  inc="V")
            s.add("gpsimd", (lambda e, k=k, c0=c0, w=w:
                             e.dma_start(out=vf_out[k * 128:(k + 1) * 128, c0:c0 + w],
                                         in_=vout_ch[k][:, :w])),
                  waits=[("V", s.v("V"))], inc="ST", inc_amt=16)
            st_war = [("ST", s.v("ST"))] if (ci > 0 or k > 0) else []
            s.add("vector", (lambda v, k=k, w=w:
                             v.tensor_copy(vb_st[:, :w], vout_ch[k][:, :w])),
                  waits=st_war, inc="V")
            s.add("gpsimd", (lambda e, k=k, c0=c0, w=w:
                             e.dma_start(out=vb_out[k * 128:(k + 1) * 128, c0:c0 + w],
                                         in_=vb_st[:, :w])),
                  waits=[("V", s.v("V"))], inc="ST", inc_amt=16)

    s.build(nc, [("ST", s.v("ST"))])
    return nc



def build_launchG():
    nc = bass.Bass("TRN2", target_bir_lowering=False)
    al = Alloc(nc)
    s = Sched()
    for nm in ("LD", "V", "S", "PE", "ST", "RD", "AC"):
        s.sem(nm)

    x2_hbm = nc.dram_tensor("x2", [D, TPC], F32, kind="ExternalInput").ap()
    v2_hbm = nc.dram_tensor("v2", [D, TPC], F32, kind="ExternalInput").ap()
    nsc_hbm = nc.dram_tensor("norm_scale", [D], F32, kind="ExternalInput").ap()
    rt_hbm = nc.dram_tensor("routerT", [D, N_EXP], F32, kind="ExternalInput").ap()
    wup_hbm = nc.dram_tensor("w_upT", [N_EXP, D, 4096], F32, kind="ExternalInput").ap()
    wdn_hbm = nc.dram_tensor("w_downT", [N_EXP, H_MOE, D], F32, kind="ExternalInput").ap()
    eye_hbm = nc.dram_tensor("eye128", [128, 128], F32, kind="ExternalInput").ap()
    selg_hbm = nc.dram_tensor("selG", [N_EXP * N_EXP, 128], F32, kind="ExternalInput").ap()
    iota_hbm = nc.dram_tensor("iota8", [128, N_EXP], F32, kind="ExternalInput").ap()
    revi_hbm = nc.dram_tensor("revi8", [128, N_EXP], F32, kind="ExternalInput").ap()
    ones_hbm = nc.dram_tensor("ones", [128, 1], F32, kind="ExternalInput").ap()
    onesr_hbm = nc.dram_tensor("onesr", [1, 128], F32, kind="ExternalInput").ap()

    out_hbm = nc.dram_tensor("out_fm", [D, TPC], F32, kind="ExternalOutput").ap()
    acc_hbm = nc.dram_tensor("acc_scratch", [D, TPC], F32, kind="Internal").ap()

    ones_col = al.sb([128, 1], F32)
    ones_row = al.sb([1, 128], F32)
    eye_sb = al.sb([128, 128], F32)
    iota_sb = al.sb([128, N_EXP], F32)
    revi_sb = al.sb([128, N_EXP], F32)
    s.add("sync", lambda e: e.dma_start(out=ones_col[:, :], in_=ones_hbm), inc="LD", inc_amt=16)
    s.add("sync", lambda e: e.dma_start(out=ones_row[:, :], in_=onesr_hbm), inc="LD", inc_amt=16)
    s.add("sync", lambda e: e.dma_start(out=eye_sb[:, :], in_=eye_hbm), inc="LD", inc_amt=16)
    s.add("sync", lambda e: e.dma_start(out=iota_sb[:, :], in_=iota_hbm), inc="LD", inc_amt=16)
    s.add("sync", lambda e: e.dma_start(out=revi_sb[:, :], in_=revi_hbm), inc="LD", inc_amt=16)
    nsc_cols = load_bias_cols(s, al, nsc_hbm, KD)
    rt_bf, _ = load_cast_weight(s, al, rt_hbm, KD, N_EXP)
    ld0 = s.v("LD")
    s.add("scalar", lambda e: e.nop(), waits=[("LD", ld0)], inc="S")
    s.add("vector", lambda v: v.nop(), waits=[("LD", ld0)], inc="V")

    xn3 = [al.sb([128, TPC], BF16) for _ in range(KD)]
    x3_ch = [al.sb([128, 512], F32) for _ in range(KD)]
    _pool = al.sb([128, 8 * 512], F32)
    ln3_ch = [_pool[:, i * 512:(i + 1) * 512] for i in range(KD)]
    stage_a = al.sb([128, 512], F32)
    lbufs = ln_bufs(al)
    for ci, (c0, w) in enumerate(t_chunks(TPC)):
        for k in range(KD):
            war = [("V", s.v("V"))]
            if ci > 0 and k == 0:
                war += [("ST", s.v("ST"))]
            s.add("sync", (lambda e, k=k, c0=c0, w=w:
                           e.dma_start(out=x3_ch[k][:, :w], in_=x2_hbm[k * 128:(k + 1) * 128, c0:c0 + w])),
                  waits=war, inc="LD", inc_amt=16)
            s.add("sync", (lambda e, k=k, c0=c0, w=w:
                           e.dma_start(out=stage_a[:, :w], in_=v2_hbm[k * 128:(k + 1) * 128, c0:c0 + w])),
                  waits=[("V", s.v("V"))], inc="LD", inc_amt=16)
            s.add("vector", (lambda v, k=k, w=w: v.tensor_add(x3_ch[k][:, :w], x3_ch[k][:, :w],
                                                              stage_a[:, :w])),
                  waits=[("LD", s.v("LD"))], inc="V")
        layernorm_fm(s, al, x3_ch, w, ln3_ch, ones_col, ones_row, bufs=lbufs)
        xn3_views = [xn3[k][:, c0:c0 + w] for k in range(KD)]
        layernorm_fm(s, al, ln3_ch, w, xn3_views, ones_col, ones_row, rms=True, eps=EPS_RMS,
                     gamma_col=nsc_cols, bufs=lbufs)
        for k in range(KD):
            s.add("vector", (lambda v, k=k, w=w: v.tensor_add(x3_ch[k][:, :w], x3_ch[k][:, :w],
                                                              ln3_ch[k][:, :w])),
                  inc="V")
            s.add("gpsimd", (lambda e, k=k, c0=c0, w=w:
                             e.dma_start(out=acc_hbm[k * 128:(k + 1) * 128, c0:c0 + w],
                                         in_=x3_ch[k][:, :w])),
                  waits=[("V", s.v("V"))], inc="ST", inc_amt=16)

    ps8 = al.ps([128, N_EXP], F32)
    sc_tm = al.sb([128, 16 * N_EXP], F32)
    w8_tm = al.sb([128, 16 * N_EXP], F32)
    t8a = al.sb([128, N_EXP], F32)
    t8b = al.sb([128, N_EXP], F32)
    m1c = al.sb([128, 1], F32)
    m2c = al.sb([128, 1], F32)
    i1c = al.sb([128, 1], F32)
    w1c = al.sb([128, 1], F32)
    w2c = al.sb([128, 1], F32)
    tps = al.ps([N_EXP, 128], F32)
    w8e = al.sb([N_EXP, TPC], F32)
    we_bc = al.sb([128, TPC], BF16)
    selg = [al.sb([N_EXP, 128], F32) for _ in range(N_EXP)]
    bc_ps = al.ps([128, 512], F32)
    for i in range(N_EXP):
        s.add("sync", (lambda e, i=i: e.dma_start(out=selg[i][:, :],
                                                  in_=selg_hbm[i * N_EXP:(i + 1) * N_EXP, :])),
              waits=[("V", s.v("V"))], inc="LD", inc_amt=16)

    for j in range(16):
        for k in range(KD):
            s.add("tensor", (lambda e, j=j, k=k:
                             e.matmul(ps8[:, :], xn3[k][:, j * 128:(j + 1) * 128], rt_bf[k][:, :],
                                      start=(k == 0), stop=(k == KD - 1))),
                  waits=[("V", s.v("V")), ("S", s.v("S"))] if k == 0 else [],
                  inc="PE" if k == KD - 1 else None)
        s.add("scalar", (lambda e, j=j: e.copy(sc_tm[:, j * N_EXP:(j + 1) * N_EXP], ps8[:, :])),
              waits=[("PE", s.v("PE"))], inc="S")
        sl = slice(j * N_EXP, (j + 1) * N_EXP)
        def vop(emit, extra=()):
            s.add("vector", emit, waits=[("V", s.v("V"))] + list(extra), inc="V")
        vop(lambda v, sl=sl: v.reduce_max(m1c[:, :], sc_tm[:, sl], axis=AX.X),
            extra=[("S", s.v("S"))])
        vop(lambda v, sl=sl: v.tensor_scalar(t8a[:, :], sc_tm[:, sl], m1c[:, :], None, OP.is_equal))
        vop(lambda v: v.tensor_mul(t8a[:, :], t8a[:, :], revi_sb[:, :]))
        vop(lambda v: v.reduce_max(i1c[:, :], t8a[:, :], axis=AX.X))
        vop(lambda v: v.tensor_scalar(i1c[:, :], i1c[:, :], -1.0, 7.0, OP.mult, OP.add))
        vop(lambda v: v.tensor_scalar(t8a[:, :], iota_sb[:, :], i1c[:, :], None, OP.is_equal))
        vop(lambda v, sl=sl: v.scalar_tensor_tensor(t8b[:, :], t8a[:, :], -1e30,
                                                    sc_tm[:, sl], OP.mult, OP.add))
        vop(lambda v: v.reduce_max(m2c[:, :], t8b[:, :], axis=AX.X))
        vop(lambda v: v.tensor_sub(w1c[:, :], m1c[:, :], m2c[:, :]))
        s.add("scalar", lambda e: e.activation(w1c[:, :], w1c[:, :], AF.Sigmoid),
              waits=[("V", s.v("V"))], inc="S")
        s.add("vector", lambda v: v.tensor_scalar(w2c[:, :], w1c[:, :], -1.0, 1.0, OP.mult, OP.add),
              waits=[("S", s.v("S")), ("V", s.v("V"))], inc="V")
        vop(lambda v: v.tensor_scalar(t8b[:, :], t8b[:, :], m2c[:, :], None, OP.is_equal))
        vop(lambda v: v.tensor_mul(t8b[:, :], t8b[:, :], revi_sb[:, :]))
        vop(lambda v: v.reduce_max(m2c[:, :], t8b[:, :], axis=AX.X))
        vop(lambda v: v.tensor_scalar(m2c[:, :], m2c[:, :], -1.0, 7.0, OP.mult, OP.add))
        vop(lambda v: v.tensor_scalar(t8b[:, :], iota_sb[:, :], m2c[:, :], None, OP.is_equal))
        vop(lambda v, sl=sl: v.tensor_scalar(w8_tm[:, sl], t8a[:, :], w1c[:, :], None, OP.mult))
        vop(lambda v, sl=sl: v.scalar_tensor_tensor(w8_tm[:, sl], t8b[:, :], w2c[:, :],
                                                    w8_tm[:, sl], OP.mult, OP.add))
        s.add("tensor", (lambda e, j=j, sl=sl: e.transpose(tps[:, :], w8_tm[:, sl], eye_sb[:, :])),
              waits=[("V", s.v("V"))], inc="PE")
        s.add("scalar", (lambda e, j=j: e.copy(w8e[:, j * 128:(j + 1) * 128], tps[:, :])),
              waits=[("PE", s.v("PE"))], inc="S")

    up_bf = [al.sb([128, 4096], BF16) for _ in range(KD)]
    dn_bf = [al.sb([128, D], BF16) for _ in range(16)]
    wstage = al.sb([128, 2048], F32)
    h_t = [_pool.ap().bitcast(BF16)[:, i * 512:(i + 1) * 512] for i in range(16)]
    ug_st = al.sb([128, 512], BF16)
    gg_st = al.sb([128, 512], BF16)
    d_st = al.sb([128, 512], F32)
    acc_ch = x3_ch
    ps = al.ps([128, 512], F32)

    for e in range(N_EXP):
        for c1, w1 in t_chunks(TPC):
            s.add("tensor", (lambda t, e=e, c1=c1, w1=w1:
                             t.matmul(bc_ps[:, :w1], selg[e][:, :],
                                      w8e[0:N_EXP, c1:c1 + w1], start=True, stop=True)),
                  waits=[("S", s.v("S")), ("V", s.v("V"))], inc="PE")
            s.add("scalar", (lambda e2, c1=c1, w1=w1: e2.copy(we_bc[:, c1:c1 + w1], bc_ps[:, :w1])),
                  waits=[("PE", s.v("PE"))], inc="S")
        for k in range(KD):
            for hh in range(2):
                war = [("V", s.v("V"))] if (k + hh + e > 0) else [("PE", s.v("PE"))]
                if k == 0 and hh == 0:
                    war += [("PE", s.v("PE"))]
                s.add("sync", (lambda e2, e=e, k=k, hh=hh:
                               e2.dma_start(out=wstage[:, :], in_=wup_hbm[e, k * 128:(k + 1) * 128,
                                                                          hh * 2048:(hh + 1) * 2048])),
                      waits=war, inc="LD", inc_amt=16)
                s.add("vector", (lambda v, k=k, hh=hh:
                                 v.tensor_copy(up_bf[k][:, hh * 2048:(hh + 1) * 2048], wstage[:, :])),
                      waits=[("LD", s.v("LD"))], inc="V")
        for k in range(16):
            s.add("sync", (lambda e2, e=e, k=k:
                           e2.dma_start(out=wstage[:, 0:D], in_=wdn_hbm[e, k * 128:(k + 1) * 128, :])),
                  waits=[("V", s.v("V"))], inc="LD", inc_amt=16)
            s.add("vector", (lambda v, k=k: v.tensor_copy(dn_bf[k][:, :], wstage[:, 0:D])),
                  waits=[("LD", s.v("LD"))], inc="V")
        w_ready_v = s.v("V")
        for ci, (c0, w) in enumerate(t_chunks(TPC)):
            for k in range(KD):
                s.add("sync", (lambda e2, k=k, c0=c0, w=w:
                               e2.dma_start(out=acc_ch[k][:, :w],
                                            in_=acc_hbm[k * 128:(k + 1) * 128, c0:c0 + w])),
                      waits=[("ST", s.v("ST")), ("V", s.v("V"))] if k == 0 else [],
                      inc="LD", inc_amt=16)
            ld_acc = s.v("LD")
            for m in range(16):
                for k in range(KD):
                    s.add("tensor", (lambda t, m=m, k=k, c0=c0, w=w:
                                     t.matmul(ps[:, :w], up_bf[k][:, m * 128:(m + 1) * 128],
                                              xn3[k][:, c0:c0 + w], start=(k == 0),
                                              stop=(k == KD - 1))),
                          waits=([("S", s.v("S"))] +
                                 ([("V", w_ready_v)] if (m == 0 and ci == 0) else []))
                                if k == 0 else [],
                          inc="PE" if k == KD - 1 else None)
                s.add("scalar", (lambda e2, w=w: e2.activation(ug_st[:, :w], ps[:, :w],
                                                               AF.Identity)),
                      waits=[("PE", s.v("PE")), ("V", s.v("V"))], inc="S")
                for k in range(KD):
                    s.add("tensor", (lambda t, m=m, k=k, c0=c0, w=w:
                                     t.matmul(ps[:, :w], up_bf[k][:, (16 + m) * 128:(17 + m) * 128],
                                              xn3[k][:, c0:c0 + w], start=(k == 0),
                                              stop=(k == KD - 1))),
                          waits=[("S", s.v("S"))] if k == 0 else [],
                          inc="PE" if k == KD - 1 else None)
                s.add("scalar", (lambda e2, w=w: e2.activation(gg_st[:, :w], ps[:, :w],
                                                               AF.Silu)),
                      waits=[("PE", s.v("PE"))], inc="S")
                s.add("vector", (lambda v, m=m, w=w:
                                 v.tensor_mul(h_t[m][:, :w], ug_st[:, :w], gg_st[:, :w])),
                      waits=[("S", s.v("S"))], inc="V")
                s.add("scalar", lambda e2: e2.nop(), waits=[("V", s.v("V"))], inc="S")
            for md in range(KD):
                for k in range(16):
                    s.add("tensor", (lambda t, md=md, k=k, w=w:
                                     t.matmul(ps[:, :w], dn_bf[k][:, md * 128:(md + 1) * 128],
                                              h_t[k][:, :w], start=(k == 0), stop=(k == 15))),
                          waits=[("S", s.v("S")), ("V", s.v("V"))] if k == 0 else [],
                          inc="PE" if k == 15 else None)
                s.add("scalar", (lambda e2, w=w: e2.activation(d_st[:, :w], ps[:, :w],
                                                               AF.Identity)),
                      waits=[("PE", s.v("PE"))], inc="S")
                s.add("vector", (lambda v, md=md, c0=c0, w=w:
                                 v.scalar_tensor_tensor(d_st[:, :w], d_st[:, :w], 1.0,
                                                        d_st[:, :w], OP.mult, OP.add)
                                 if False else
                                 v.tensor_mul(d_st[:, :w], d_st[:, :w], we_bc[:, c0:c0 + w])),
                      waits=[("S", s.v("S")), ("LD", ld_acc)], inc="V")
                s.add("vector", (lambda v, md=md, w=w:
                                 v.tensor_add(acc_ch[md][:, :w], acc_ch[md][:, :w], d_st[:, :w])),
                      inc="V")
                s.add("scalar", lambda e2: e2.nop(), waits=[("V", s.v("V"))], inc="S")
            dst = acc_hbm if e < N_EXP - 1 else out_hbm
            for k in range(KD):
                s.add("gpsimd", (lambda e2, k=k, c0=c0, w=w, dst=dst:
                                 e2.dma_start(out=dst[k * 128:(k + 1) * 128, c0:c0 + w],
                                              in_=acc_ch[k][:, :w])),
                      waits=[("V", s.v("V"))], inc="ST", inc_amt=16)

    s.build(nc, [("ST", s.v("ST"))])
    return nc



def _win_idx(t0):
    return (np.arange(-HALO, TPC) + t0) % T


_ONES_COL = np.ones((128, 1), np.float32)
_ONES_ROW = np.ones((1, 128), np.float32)


def _mk_selbc():
    sel = np.zeros((32, 96, 128), np.float32)
    for n in range(16):
        sel[n, 64 + n, :] = 1.0
        sel[16 + n, 80 + n, :] = 1.0
    return sel.reshape(32 * 96, 128)


def _mk_selg():
    sel = np.zeros((N_EXP, N_EXP, 128), np.float32)
    for e in range(N_EXP):
        sel[e, e, :] = 1.0
    return sel.reshape(N_EXP * N_EXP, 128)


_SELBC = _mk_selbc()
_SELG = _mk_selg()


def _run(build_fn, in_maps):
    nc = build_fn()
    return run_bass_kernel_spmd(nc, in_maps, core_ids=list(range(N_CORES)))


def kernel(_debug=None, **inp):
    inp = {k: np.asarray(v) for k, v in inp.items()}
    x = inp["L0_attn_x"]

    xw = []
    for c in range(N_CORES):
        b, q = c // 4, c % 4
        idx = _win_idx(q * TPC)
        xw.append(np.ascontiguousarray(x[b, idx, :].T))

    resA = _run(build_launchA, [{
        "x_win": xw[c],
        "in_w": inp["L0_attn_in_w"],
        "in_b": inp["L0_attn_in_b"],
        "w1": inp["L0_attn_mod_w1"],
        "b1": inp["L0_attn_mod_b1"],
        "ones": _ONES_COL, "onesr": _ONES_ROW,
    } for c in range(N_CORES)])

    resB1 = _run(build_launchB1, [{
        "x_proj": resA.results[c]["x_proj"],
        "g": resA.results[c]["g"],
        "w2": inp["L0_attn_mod_w2"],
        "b2": inp["L0_attn_mod_b2"],
    } for c in range(N_CORES)])

    resB2 = _run(build_launchB2, [{
        "m": resB1.results[c]["m"],
        "ker": inp["L0_attn_kernel"],
        "out_w": inp["L0_attn_out_w"],
        "out_b": inp["L0_attn_out_b"],
        "x_win": xw[c],
        "mu_r": inp["L0_ffn_mu_r"],
        "mu_k": inp["L0_ffn_mu_k"],
        "first_mask": np.full((128, 1), 0.0 if c % 4 == 0 else 1.0, np.float32),
        "ones": _ONES_COL, "onesr": _ONES_ROW,
    } for c in range(N_CORES)])

    WrT = np.ascontiguousarray(inp["L0_ffn_W_r"].T)
    WkT = np.ascontiguousarray(inp["L0_ffn_W_k"].T)
    WvT = np.ascontiguousarray(inp["L0_ffn_W_v"].T)
    resC = _run(build_launchC, [{
        "xr": resB2.results[c]["xr"],
        "xk": resB2.results[c]["xk"],
        "x1": resB2.results[c]["x1"],
        "WrT": WrT, "WkT": WkT, "WvT": WvT,
        "ones": _ONES_COL, "onesr": _ONES_ROW,
    } for c in range(N_CORES)])

    def mamba_layer(l, v_shards_bf, v_shards_res, vin_f32):
        in_wT = inp["L1_attn_in_proj_w"][l].T
        xp_wT = inp["L1_attn_x_proj_w"][l].T
        dt_wT = inp["L1_attn_dt_proj_w"][l].T
        out_wT = np.ascontiguousarray(inp["L1_attn_out_proj_w"][l].T)
        v_full = [np.concatenate([v_shards_bf[4 * b + j] for j in range(4)], axis=1)
                  for b in range(2)]
        mapsD = []
        for c in range(N_CORES):
            b, q = c // 4, c % 4
            sl = q * DSL
            mapsD.append({
                "v_full": v_full[b],
                "in_wT_sl": np.ascontiguousarray(
                    np.concatenate([in_wT[:, sl:sl + DSL],
                                    in_wT[:, D_INNER + sl:D_INNER + sl + DSL]], axis=1)),
                "conv_w_sl": np.ascontiguousarray(inp["L1_attn_conv_w"][l][sl:sl + DSL, :]),
                "conv_b_sl": np.ascontiguousarray(inp["L1_attn_conv_b"][l][sl:sl + DSL]),
                "xp_wT_sl": np.ascontiguousarray(xp_wT[sl:sl + DSL, :]),
            })
        resD = _run(build_launchD, mapsD)

        mapsE = []
        for c in range(N_CORES):
            b, q = c // 4, c % 4
            sl = q * DSL
            mapsE.append({
                "u": resD.results[c]["u"],
                "z": resD.results[c]["z"],
                "xdbl_all": np.stack([resD.results[4 * b + j]["xdbl_p"] for j in range(4)]),
                "dt_wT_sl": np.ascontiguousarray(dt_wT[:, sl:sl + DSL]),
                "dt_b_sl": np.ascontiguousarray(inp["L1_attn_dt_proj_b"][l][sl:sl + DSL]),
                "A_log_sl": np.ascontiguousarray(inp["L1_attn_A_log"][l][sl:sl + DSL, :]),
                "Dskip_sl": np.ascontiguousarray(inp["L1_attn_Dskip"][l][sl:sl + DSL]),
                "onesr": _ONES_ROW,
                "selBC": _SELBC,
            })
        resE = _run(build_launchE, mapsE)

        y_full = [np.concatenate([resE.results[4 * b + j]["y"] for j in range(4)], axis=0)
                  for b in range(2)]
        mapsF = []
        for c in range(N_CORES):
            b, q = c // 4, c % 4
            mapsF.append({
                "y_b": np.ascontiguousarray(y_full[b][:, q * TPC:(q + 1) * TPC]),
                "out_wT": out_wT,
                "ln_w": np.ascontiguousarray(inp["L1_attn_ln_w"][l]),
                "ln_b": np.ascontiguousarray(inp["L1_attn_ln_b"][l]),
                "v_in": v_shards_res[c],
                "ones": _ONES_COL, "onesr": _ONES_ROW,
            })
        resF = _run(lambda: build_launchF(vin_f32), mapsF)
        if _debug is not None:
            _debug[f"D{l}"] = resD
            _debug[f"E{l}"] = resE
            _debug[f"F{l}"] = resF
        return ([resF.results[c]["v_f"] for c in range(N_CORES)],
                [resF.results[c]["v_b"] for c in range(N_CORES)])

    v0_shards = [resC.results[c]["v0"] for c in range(N_CORES)]
    v1f, v1b = mamba_layer(0, v0_shards, v0_shards, vin_f32=False)
    v2f, v2b = mamba_layer(1, v1b, v1f, vin_f32=True)

    w_upT = np.ascontiguousarray(np.transpose(inp["L1_ffn_w_up"], (0, 2, 1)))
    w_downT = np.ascontiguousarray(np.transpose(inp["L1_ffn_w_down"], (0, 2, 1)))
    routerT = np.ascontiguousarray(inp["L1_ffn_w_router"].T)
    eye128 = np.eye(128, dtype=np.float32)
    iota8 = np.tile(np.arange(8, dtype=np.float32), (128, 1))
    revi8 = np.tile(7.0 - np.arange(8, dtype=np.float32), (128, 1))
    resG = _run(build_launchG, [{
        "x2": resC.results[c]["x2"],
        "v2": v2f[c],
        "norm_scale": inp["L1_ffn_norm_scale"],
        "routerT": routerT,
        "w_upT": w_upT,
        "w_downT": w_downT,
        "eye128": eye128, "iota8": iota8, "revi8": revi8, "selG": _SELG,
        "ones": _ONES_COL, "onesr": _ONES_ROW,
    } for c in range(N_CORES)])

    if _debug is not None:
        _debug.update(A=resA, B1=resB1, B2=resB2, C=resC, G=resG)

    out = np.empty((B, T, D), np.float32)
    for c in range(N_CORES):
        b, q = c // 4, c % 4
        out[b, q * TPC:(q + 1) * TPC, :] = resG.results[c]["out_fm"].T
    return out


if __name__ == "__main__":
    pass


# version 03
# speedup vs baseline: 1.3736x; 1.3736x over previous
import contextlib
import numpy as np

import concourse.bass as bass
import concourse.mybir as mybir
from concourse.bass_utils import run_bass_kernel_spmd

F32 = mybir.dt.float32
BF16 = mybir.dt.bfloat16
AF = mybir.ActivationFunctionType
OP = mybir.AluOpType
AX = mybir.AxisListType

N_CORES = 8
B, T, D = 2, 8192, 1024
TPC = 2048
HALO = 64
WIN = TPC + HALO
KD = D // 128
EPS_LN = 1e-5
EPS_RMS = 1e-6

D_INNER = 2048
D_STATE = 16
DT_RANK = 64
D_CONV = 4
DSL = 512
N_EXP = 8
H_MOE = 2048


ENGINES = ("sync", "vector", "scalar", "tensor", "gpsimd")


class Sched:

    def __init__(self):
        self.q = {e: [] for e in ENGINES}
        self.n = {}
        self.sem_names = []

    def sem(self, name):
        if name not in self.n:
            self.n[name] = 0
            self.sem_names.append(name)
        return name

    def add(self, eng, emit, waits=(), inc=None, inc_amt=1):
        waits = tuple((sm, v) for sm, v in waits if v > 0)
        if inc is not None:
            self.sem(inc)
            self.n[inc] += inc_amt
        self.q[eng].append((emit, waits, inc, inc_amt))
        return self.n[inc] if inc is not None else None

    def v(self, sem):
        return self.n.get(sem, 0)

    def build(self, nc, final_waits):
        with contextlib.ExitStack() as st:
            handles = {name: st.enter_context(nc.semaphore(name)) for name in self.sem_names}
            block = st.enter_context(nc.Block())

            def emit_engine(eng_obj, eng_name):
                for emit, waits, inc, inc_amt in self.q[eng_name]:
                    for sm, v in waits:
                        eng_obj.wait_ge(handles[sm], v)
                    inst = emit(eng_obj)
                    if inc is not None:
                        inst.then_inc(handles[inc], inc_amt)
                if eng_name == "gpsimd":
                    for sm, v in final_waits:
                        eng_obj.wait_ge(handles[sm], v)

            @block.sync
            def _(e):
                emit_engine(e, "sync")

            @block.vector
            def _(e):
                emit_engine(e, "vector")

            @block.scalar
            def _(e):
                emit_engine(e, "scalar")

            @block.tensor
            def _(e):
                emit_engine(e, "tensor")

            @block.gpsimd
            def _(e):
                emit_engine(e, "gpsimd")


class Alloc:
    def __init__(self, nc):
        self.nc = nc
        self.i = 0

    def sb(self, shape, dtype, name=None):
        self.i += 1
        return self.nc.alloc_sbuf_tensor(name or f"sb{self.i}", shape, dtype)

    def ps(self, shape, dtype=F32, name=None):
        self.i += 1
        return self.nc.alloc_psum_tensor(name or f"ps{self.i}", shape, dtype)


def t_chunks(total, step=512):
    out = []
    c0 = 0
    while c0 < total:
        w = min(step, total - c0)
        out.append((c0, w))
        c0 += w
    return out



def load_cast_weight(s, al, src_ap, kd, width, tiles=None, stage=None, dst_dtype=BF16,
                     extra_war=()):
    if stage is None:
        stage = al.sb([128, width], F32)
    if tiles is None:
        tiles = [al.sb([128, width], dst_dtype) for _ in range(kd)]
    for k in range(kd):
        war = [("V", s.v("V"))] + (list(extra_war) if k == 0 else [])
        s.add("sync", (lambda e, k=k, stage=stage, ap=src_ap, w=width:
                       e.dma_start(out=stage[:, :w], in_=ap[k * 128:(k + 1) * 128, :])),
              waits=war, inc="LD", inc_amt=16)
        s.add("vector", (lambda v, k=k, tiles=tiles, stage=stage, w=width:
                         v.tensor_copy(tiles[k][:, :w], stage[:, :w])),
              waits=[("LD", s.v("LD"))], inc="V")
    return tiles, stage


def load_bias_cols(s, al, bias_ap, kd):
    cols = []
    for k in range(kd):
        col = al.sb([128, 1], F32)
        s.add("sync", (lambda e, k=k, col=col, ap=bias_ap:
                       e.dma_start(out=col[:, :], in_=ap[k * 128:(k + 1) * 128].unsqueeze(1))),
              inc="LD", inc_amt=16)
        cols.append(col)
    return cols


def ln_bufs(al):
    return dict(
        sq=al.sb([128, 512], F32),
        stat=al.ps([1, 512], F32),
        mu_row=al.sb([1, 512], F32),
        msq_row=al.sb([1, 512], F32),
        rstd_row=al.sb([1, 512], F32),
        tmp_row=al.sb([1, 512], F32),
        bc_ps=al.ps([128, 512], F32),
        rstd_b=al.sb([128, 512], F32),
        mur_b=al.sb([128, 512], F32),
    )


def layernorm_fm(s, al, x_tiles, width, out_tiles, ones_col, ones_row,
                 gamma_col=None, beta_col=None, rms=False, eps=EPS_LN, nd=None,
                 bufs=None):
    kd = len(x_tiles)
    if nd is None:
        nd = kd * 128
    if bufs is None:
        bufs = ln_bufs(al)
    sq, stat, mu_row, msq_row = bufs["sq"], bufs["stat"], bufs["mu_row"], bufs["msq_row"]
    rstd_row, tmp_row, bc_ps = bufs["rstd_row"], bufs["tmp_row"], bufs["bc_ps"]
    rstd_b, mur_b = bufs["rstd_b"], bufs["mur_b"]

    for c0, w in t_chunks(width):
        if not rms:
            for k in range(kd):
                s.add("tensor", (lambda e, k=k, c0=c0, w=w:
                                 e.matmul(stat[:, :w], ones_col[:, :], x_tiles[k][:, c0:c0 + w],
                                          start=(k == 0), stop=(k == kd - 1))),
                      waits=[("V", s.v("V")), ("S", s.v("S"))] if k == 0 else [],
                      inc="PE" if k == kd - 1 else None)
            s.add("vector", (lambda v, w=w:
                             v.tensor_scalar_mul(mu_row[:, :w], stat[:, :w], 1.0 / nd)),
                  waits=[("PE", s.v("PE"))], inc="V")
        for k in range(kd):
            s.add("scalar", (lambda e, k=k, c0=c0, w=w:
                             e.activation(sq[:, :w], x_tiles[k][:, c0:c0 + w], AF.Square)),
                  waits=[("PE", s.v("PE"))] if k > 0 else [("V", s.v("V"))],
                  inc="S")
            s.add("tensor", (lambda e, k=k, w=w:
                             e.matmul(stat[:, :w], ones_col[:, :], sq[:, :w],
                                      start=(k == 0), stop=(k == kd - 1))),
                  waits=([("S", s.v("S")), ("V", s.v("V"))] if k == 0 else [("S", s.v("S"))]),
                  inc="PE")
        s.add("vector", (lambda v, w=w:
                         v.tensor_scalar_mul(msq_row[:, :w], stat[:, :w], 1.0 / nd)),
              waits=[("PE", s.v("PE"))], inc="V")
        if not rms:
            s.add("vector", (lambda v, w=w: v.tensor_mul(tmp_row[:, :w], mu_row[:, :w], mu_row[:, :w])),
                  inc="V")
            s.add("vector", (lambda v, w=w: v.tensor_sub(msq_row[:, :w], msq_row[:, :w], tmp_row[:, :w])),
                  inc="V")
        s.add("vector", (lambda v, w=w: v.tensor_scalar_add(msq_row[:, :w], msq_row[:, :w],
                                                            float(eps))),
              inc="V")
        s.add("scalar", (lambda e, w=w: e.activation(tmp_row[:, :w], msq_row[:, :w], AF.Sqrt)),
              waits=[("V", s.v("V"))], inc="S")
        s.add("vector", (lambda v, w=w: v.reciprocal(rstd_row[:, :w], tmp_row[:, :w])),
              waits=[("S", s.v("S"))], inc="V")
        if not rms:
            s.add("vector", (lambda v, w=w: v.tensor_mul(tmp_row[:, :w], mu_row[:, :w], rstd_row[:, :w])),
                  inc="V")
        s.add("tensor", (lambda e, w=w:
                         e.matmul(bc_ps[:, :w], ones_row[:, :], rstd_row[0:1, :w],
                                  start=True, stop=True)),
              waits=[("V", s.v("V")), ("S", s.v("S"))], inc="PE")
        s.add("scalar", (lambda e, w=w: e.copy(rstd_b[:, :w], bc_ps[:, :w])),
              waits=[("PE", s.v("PE"))], inc="S")
        if not rms:
            s.add("tensor", (lambda e, w=w:
                             e.matmul(bc_ps[:, :w], ones_row[:, :], tmp_row[0:1, :w],
                                      start=True, stop=True)),
                  waits=[("S", s.v("S"))], inc="PE")
            s.add("scalar", (lambda e, w=w: e.copy(mur_b[:, :w], bc_ps[:, :w])),
                  waits=[("PE", s.v("PE"))], inc="S")
        sden = [("S", s.v("S"))]
        for k in range(kd):
            if rms:
                if gamma_col is not None:
                    s.add("vector", (lambda v, k=k, c0=c0, w=w:
                                     v.tensor_mul(sq[:, :w], x_tiles[k][:, c0:c0 + w], rstd_b[:, :w])),
                          waits=sden if k == 0 else [], inc="V")
                    s.add("vector", (lambda v, k=k, c0=c0, w=w:
                                     v.tensor_scalar(out_tiles[k][:, c0:c0 + w], sq[:, :w],
                                                     gamma_col[k][:, :], None, OP.mult)),
                          inc="V")
                else:
                    s.add("vector", (lambda v, k=k, c0=c0, w=w:
                                     v.tensor_mul(out_tiles[k][:, c0:c0 + w],
                                                  x_tiles[k][:, c0:c0 + w], rstd_b[:, :w])),
                          waits=sden if k == 0 else [], inc="V")
            else:
                s.add("vector", (lambda v, k=k, c0=c0, w=w:
                                 v.tensor_mul(sq[:, :w], x_tiles[k][:, c0:c0 + w], rstd_b[:, :w])),
                      waits=sden if k == 0 else [], inc="V")
                if gamma_col is None:
                    s.add("vector", (lambda v, k=k, c0=c0, w=w:
                                     v.tensor_sub(out_tiles[k][:, c0:c0 + w], sq[:, :w], mur_b[:, :w])),
                          inc="V")
                else:
                    s.add("vector", (lambda v, k=k, w=w:
                                     v.tensor_sub(sq[:, :w], sq[:, :w], mur_b[:, :w])),
                          inc="V")
                    s.add("vector", (lambda v, k=k, c0=c0, w=w:
                                     v.tensor_scalar(out_tiles[k][:, c0:c0 + w], sq[:, :w],
                                                     gamma_col[k][:, :], None, OP.mult)),
                          inc="V")
                    s.add("vector", (lambda v, k=k, c0=c0, w=w:
                                     v.tensor_scalar(out_tiles[k][:, c0:c0 + w],
                                                     out_tiles[k][:, c0:c0 + w],
                                                     beta_col[k][:, :], None, OP.add)),
                          inc="V")
    return bufs


def gemm_store(s, al, name, xn_tiles, w_bf, n_out_tiles, width, out_hbm, ps, bias_cols=None,
               act=AF.Identity, out_tile=None, m_off=0, extra_first_waits=()):
    if out_tile is None:
        out_tile = al.sb([128, 1024], F32)
    kd = len(xn_tiles)
    chunks = t_chunks(width)
    ps2 = ps if isinstance(ps, list) else [ps, ps]
    s_evac = [0, 0]
    gi = 0
    for m in range(n_out_tiles):
        for ci, (c0, w) in enumerate(chunks):
            half = ci % 2
            par = gi % 2
            gi += 1
            for k in range(kd):
                waits = []
                if k == 0:
                    waits = [("S", s_evac[par])]
                    if m == 0 and ci == 0:
                        waits += [("V", s.v("V")), ("S", s.v("S"))] + list(extra_first_waits)
                s.add("tensor", (lambda e, m=m, k=k, c0=c0, w=w, par=par:
                                 e.matmul(ps2[par][:, :w], w_bf[k][:, m * 128:(m + 1) * 128],
                                          xn_tiles[k][:, c0:c0 + w],
                                          start=(k == 0), stop=(k == kd - 1))),
                      waits=waits, inc="PE" if k == kd - 1 else None)
            bias = bias_cols[m_off + m][:, :] if bias_cols is not None else 0.0
            st_war = [("ST", s.v("ST") - 16)] if ci >= 2 else []
            s.add("scalar", (lambda e, half=half, c0=c0, w=w, bias=bias, par=par:
                             e.activation(out_tile[:, half * 512:half * 512 + w], ps2[par][:, :w],
                                          act, bias=bias)),
                  waits=[("PE", s.v("PE"))] + st_war, inc="S")
            s_evac[par] = s.v("S")
            s.add("gpsimd", (lambda e, m=m, half=half, c0=c0, w=w:
                             e.dma_start(out=out_hbm[(m_off + m) * 128:(m_off + m + 1) * 128,
                                                     c0:c0 + w],
                                         in_=out_tile[:, half * 512:half * 512 + w])),
                  waits=[("S", s.v("S"))], inc="ST", inc_amt=16)
        s.add("scalar", lambda e: e.nop(), waits=[("ST", s.v("ST"))], inc="S")



def build_launchA():
    nc = bass.Bass("TRN2", target_bir_lowering=False)
    al = Alloc(nc)
    s = Sched()
    for nm in ("LD", "V", "S", "PE", "ST"):
        s.sem(nm)

    x_hbm = nc.dram_tensor("x_win", [D, WIN], F32, kind="ExternalInput").ap()
    inw_hbm = nc.dram_tensor("in_w", [D, D], F32, kind="ExternalInput").ap()
    inb_hbm = nc.dram_tensor("in_b", [D], F32, kind="ExternalInput").ap()
    w1_hbm = nc.dram_tensor("w1", [D, 4096], F32, kind="ExternalInput").ap()
    b1_hbm = nc.dram_tensor("b1", [4096], F32, kind="ExternalInput").ap()
    ones_hbm = nc.dram_tensor("ones", [128, 1], F32, kind="ExternalInput").ap()
    onesr_hbm = nc.dram_tensor("onesr", [1, 128], F32, kind="ExternalInput").ap()

    xproj_out = nc.dram_tensor("x_proj", [D, WIN], F32, kind="ExternalOutput").ap()
    g_out = nc.dram_tensor("g", [4096, WIN], F32, kind="ExternalOutput").ap()

    x_tiles = [al.sb([128, WIN], F32) for _ in range(KD)]
    for k in range(KD):
        s.add("sync", (lambda e, k=k: e.dma_start(out=x_tiles[k][:, :],
                                                  in_=x_hbm[k * 128:(k + 1) * 128, :])),
              inc="LD", inc_amt=16)
    ones_col = al.sb([128, 1], F32)
    ones_row = al.sb([1, 128], F32)
    s.add("sync", lambda e: e.dma_start(out=ones_col[:, :], in_=ones_hbm), inc="LD", inc_amt=16)
    s.add("sync", lambda e: e.dma_start(out=ones_row[:, :], in_=onesr_hbm), inc="LD", inc_amt=16)
    inb_cols = load_bias_cols(s, al, inb_hbm, KD)
    b1_cols = load_bias_cols(s, al, b1_hbm, 32)
    ld_all = s.v("LD")

    s.add("vector", lambda v: v.nop(), waits=[("LD", ld_all)], inc="V")
    s.add("scalar", lambda e: e.nop(), waits=[("LD", ld_all)], inc="S")

    xn = [al.sb([128, WIN], BF16) for _ in range(KD)]
    layernorm_fm(s, al, x_tiles, WIN, xn, ones_col, ones_row)

    inw_bf, wstage = load_cast_weight(s, al, inw_hbm, KD, D)
    ps = [al.ps([128, 512], F32), al.ps([128, 512], F32)]
    gemm_store(s, al, "xproj", xn, inw_bf, KD, WIN, xproj_out, ps, bias_cols=inb_cols,
               act=AF.Identity)

    w1_tiles = [al.sb([128, 2048], BF16) for _ in range(KD)]
    w1_stage = al.sb([128, 2048], F32)
    for half in range(2):
        war = [("PE", s.v("PE"))] if half > 0 else []
        load_cast_weight(s, al, w1_hbm[:, half * 2048:(half + 1) * 2048], KD, 2048,
                         tiles=w1_tiles, stage=w1_stage, extra_war=war)
        gemm_store(s, al, f"g{half}", xn, w1_tiles, 16, WIN, g_out, ps, bias_cols=b1_cols,
                   act=AF.Gelu, m_off=half * 16)

    s.build(nc, [("ST", s.v("ST"))])
    return nc



def build_launchB1():
    nc = bass.Bass("TRN2", target_bir_lowering=False)
    al = Alloc(nc)
    s = Sched()
    for nm in ("LD", "V", "S", "PE", "ST"):
        s.sem(nm)

    xproj_hbm = nc.dram_tensor("x_proj", [D, WIN], F32, kind="ExternalInput").ap()
    g_hbm = nc.dram_tensor("g", [4096, WIN], F32, kind="ExternalInput").ap()
    w2_hbm = nc.dram_tensor("w2", [4096, D], F32, kind="ExternalInput").ap()
    b2_hbm = nc.dram_tensor("b2", [D], F32, kind="ExternalInput").ap()
    m_out = nc.dram_tensor("m", [D, WIN], BF16, kind="ExternalOutput").ap()

    b2_cols = load_bias_cols(s, al, b2_hbm, KD)
    xp_tiles = [al.sb([128, WIN], F32) for _ in range(KD)]
    for k in range(KD):
        s.add("sync", (lambda e, k=k: e.dma_start(out=xp_tiles[k][:, :],
                                                  in_=xproj_hbm[k * 128:(k + 1) * 128, :])),
              inc="LD", inc_amt=16)
    w2_bf, _ = load_cast_weight(s, al, w2_hbm, 32, D)
    ld_w2_done_v = s.v("V")

    g_bf = [al.sb([128, 512], BF16) for _ in range(32)]
    g_stage = al.sb([128, 512], F32)
    psb = [al.ps([128, 512], F32), al.ps([128, 512], F32)]
    mod_tile2 = [al.sb([128, 512], F32), al.sb([128, 512], F32)]
    m_stage = al.sb([128, 1024], BF16)
    s_evac_b1 = [0, 0]

    for ci, (c0, w) in enumerate(t_chunks(WIN)):
        for k in range(32):
            war = [("V", s.v("V"))]
            if k == 0 and ci > 0:
                war += [("PE", s.v("PE"))]
            s.add("sync", (lambda e, k=k, c0=c0, w=w:
                           e.dma_start(out=g_stage[:, :w], in_=g_hbm[k * 128:(k + 1) * 128, c0:c0 + w])),
                  waits=war, inc="LD", inc_amt=16)
            s.add("vector", (lambda v, k=k, w=w: v.tensor_copy(g_bf[k][:, :w], g_stage[:, :w])),
                  waits=[("LD", s.v("LD"))], inc="V")
        for m in range(KD):
            par = m % 2
            for k in range(32):
                waits = []
                if k == 0:
                    waits = [("S", s_evac_b1[par]), ("V", s.v("V"))]
                s.add("tensor", (lambda e, m=m, k=k, w=w, par=par:
                                 e.matmul(psb[par][:, :w], w2_bf[k][:, m * 128:(m + 1) * 128],
                                          g_bf[k][:, :w], start=(k == 0), stop=(k == 31))),
                      waits=waits, inc="PE" if k == 31 else None)
            s.add("scalar", (lambda e, m=m, w=w, par=par:
                             e.activation(mod_tile2[par][:, :w], psb[par][:, :w], AF.Identity,
                                          bias=b2_cols[m][:, :])),
                  waits=[("PE", s.v("PE"))], inc="S")
            s_evac_b1[par] = s.v("S")
            half = m % 2
            st_war = [("ST", s.v("ST") - 16)] if (ci * KD + m) >= 2 else []
            s.add("vector", (lambda v, m=m, half=half, c0=c0, w=w, par=par:
                             v.tensor_mul(m_stage[:, half * 512:half * 512 + w],
                                          xp_tiles[m][:, c0:c0 + w], mod_tile2[par][:, :w])),
                  waits=[("S", s.v("S"))] + st_war, inc="V")
            s.add("gpsimd", (lambda e, m=m, half=half, c0=c0, w=w:
                             e.dma_start(out=m_out[m * 128:(m + 1) * 128, c0:c0 + w],
                                         in_=m_stage[:, half * 512:half * 512 + w])),
                  waits=[("V", s.v("V"))], inc="ST", inc_amt=16)
            s.add("scalar", lambda e: e.nop(), waits=[("V", s.v("V"))], inc="S")

    s.build(nc, [("ST", s.v("ST"))])
    return nc



def build_launchB2():
    nc = bass.Bass("TRN2", target_bir_lowering=False)
    al = Alloc(nc)
    s = Sched()
    for nm in ("LD", "V", "S", "PE", "ST", "GP"):
        s.sem(nm)

    m_hbm = nc.dram_tensor("m", [D, WIN], BF16, kind="ExternalInput").ap()
    ker_hbm = nc.dram_tensor("ker", [D, 64], F32, kind="ExternalInput").ap()
    outw_hbm = nc.dram_tensor("out_w", [D, D], F32, kind="ExternalInput").ap()
    outb_hbm = nc.dram_tensor("out_b", [D], F32, kind="ExternalInput").ap()
    x_hbm = nc.dram_tensor("x_win", [D, WIN], F32, kind="ExternalInput").ap()
    mur_hbm = nc.dram_tensor("mu_r", [D], F32, kind="ExternalInput").ap()
    muk_hbm = nc.dram_tensor("mu_k", [D], F32, kind="ExternalInput").ap()
    mask_hbm = nc.dram_tensor("first_mask", [128, 1], F32, kind="ExternalInput").ap()
    ones_hbm = nc.dram_tensor("ones", [128, 1], F32, kind="ExternalInput").ap()
    onesr_hbm = nc.dram_tensor("onesr", [1, 128], F32, kind="ExternalInput").ap()
    eye_hbm = nc.dram_tensor("eye128", [128, 128], F32, kind="ExternalInput").ap()

    x1_out = nc.dram_tensor("x1", [D, TPC], F32, kind="ExternalOutput").ap()
    xr_out = nc.dram_tensor("xr", [D, TPC], BF16, kind="ExternalOutput").ap()
    xk_out = nc.dram_tensor("xk", [D, TPC], BF16, kind="ExternalOutput").ap()

    VW = TPC + 1

    m_tiles = [al.sb([128, WIN], BF16) for _ in range(KD)]
    for k in range(KD):
        s.add("sync", (lambda e, k=k: e.dma_start(out=m_tiles[k][:, :],
                                                  in_=m_hbm[k * 128:(k + 1) * 128, :])),
              inc="LD", inc_amt=16)
    ker_tiles = [al.sb([128, 64], F32) for _ in range(KD)]
    for k in range(KD):
        s.add("sync", (lambda e, k=k: e.dma_start(out=ker_tiles[k][:, :],
                                                  in_=ker_hbm[k * 128:(k + 1) * 128, :])),
              inc="LD", inc_amt=16)
    ones_col = al.sb([128, 1], F32)
    ones_row = al.sb([1, 128], F32)
    mask_col = al.sb([128, 1], F32)
    s.add("sync", lambda e: e.dma_start(out=ones_col[:, :], in_=ones_hbm), inc="LD", inc_amt=16)
    s.add("sync", lambda e: e.dma_start(out=ones_row[:, :], in_=onesr_hbm), inc="LD", inc_amt=16)
    s.add("sync", lambda e: e.dma_start(out=mask_col[:, :], in_=mask_hbm), inc="LD", inc_amt=16)
    outb_cols = load_bias_cols(s, al, outb_hbm, KD)
    mur_cols = load_bias_cols(s, al, mur_hbm, KD)
    muk_cols = load_bias_cols(s, al, muk_hbm, KD)
    outw_bf, _ = load_cast_weight(s, al, outw_hbm, KD, D)
    ld_all = s.v("LD")
    s.add("vector", lambda v: v.nop(), waits=[("LD", ld_all)], inc="V")
    s.add("scalar", lambda e: e.nop(), waits=[("LD", ld_all)], inc="S")
    s.add("gpsimd", lambda e: e.nop(), waits=[("LD", ld_all)], inc="GP")

    eye_f = al.sb([128, 128], F32)
    eye_bf = al.sb([128, 128], BF16)
    s.add("sync", lambda e: e.dma_start(out=eye_f[:, :], in_=eye_hbm), inc="LD", inc_amt=16)
    s.add("vector", lambda v: v.tensor_copy(eye_bf[:, :], eye_f[:, :]),
          waits=[("LD", s.v("LD"))], inc="V")
    diag = [al.sb([128, 128], BF16) for _ in range(64)]
    y_tiles = [al.sb([128, VW], BF16) for _ in range(KD)]
    ps_conv = al.ps([128, 512], F32)
    for dk in range(KD):
        for j in range(64):
            s.add("vector", (lambda v, dk=dk, j=j:
                             v.tensor_scalar(diag[j][:, :], eye_bf[:, :],
                                             ker_tiles[dk][:, j:j + 1], None, OP.mult)),
                  waits=[("PE", s.v("PE"))] if j == 0 else [], inc="V")
        dg_v = s.v("V")
        for ci, (c0, w) in enumerate(t_chunks(VW)):
            for j in range(64):
                waits = []
                if j == 0:
                    waits = [("S", s.v("S")), ("V", dg_v)]
                s.add("tensor", (lambda e, dk=dk, j=j, c0=c0, w=w:
                                 e.matmul(ps_conv[:, :w], diag[j][:, :],
                                          m_tiles[dk][:, 63 + c0 - j:63 + c0 - j + w],
                                          start=(j == 0), stop=(j == 63))),
                      waits=waits, inc="PE" if j == 63 else None)
            s.add("scalar", (lambda e, dk=dk, c0=c0, w=w:
                             e.activation(y_tiles[dk][:, c0:c0 + w], ps_conv[:, :w], AF.Identity)),
                  waits=[("PE", s.v("PE"))], inc="S")

    ps = al.ps([128, 512], F32)
    x1f = [al.sb([128, VW], F32) for _ in range(KD)]
    attn_tile = al.sb([128, 512], F32)
    x_stage = al.sb([128, VW], F32)
    for mm in range(KD):
        s.add("sync", (lambda e, mm=mm: e.dma_start(out=x_stage[:, :],
                                                    in_=x_hbm[mm * 128:(mm + 1) * 128, 63:WIN])),
              waits=[("V", s.v("V"))], inc="LD", inc_amt=16)
        ld_x = s.v("LD")
        for ci, (c0, w) in enumerate(t_chunks(VW)):
            for k in range(KD):
                waits = []
                if k == 0:
                    waits = [("S", s.v("S"))]
                    if mm == 0 and ci == 0:
                        waits += [("V", s.v("V"))]
                s.add("tensor", (lambda e, mm=mm, k=k, c0=c0, w=w:
                                 e.matmul(ps[:, :w], outw_bf[k][:, mm * 128:(mm + 1) * 128],
                                          y_tiles[k][:, c0:c0 + w], start=(k == 0), stop=(k == KD - 1))),
                      waits=waits, inc="PE" if k == KD - 1 else None)
            s.add("scalar", (lambda e, mm=mm, w=w:
                             e.activation(attn_tile[:, :w], ps[:, :w], AF.Identity,
                                          bias=outb_cols[mm][:, :])),
                  waits=[("PE", s.v("PE")), ("LD", ld_x)], inc="S")
            s.add("vector", (lambda v, mm=mm, c0=c0, w=w:
                             v.tensor_add(x1f[mm][:, c0:c0 + w], x_stage[:, c0:c0 + w],
                                          attn_tile[:, :w])),
                  waits=[("S", s.v("S"))], inc="V")
            s.add("scalar", lambda e: e.nop(), waits=[("V", s.v("V"))], inc="S")

    for k in range(KD):
        s.add("gpsimd", (lambda e, k=k: e.dma_start(out=x1_out[k * 128:(k + 1) * 128, :],
                                                    in_=x1f[k][:, 1:VW])),
              waits=[("V", s.v("V"))], inc="ST", inc_amt=16)

    xn1 = [m_tiles[k][:, 0:VW] for k in range(KD)]
    layernorm_fm(s, al, x1f, VW, xn1, ones_col, ones_row)
    for k in range(KD):
        s.add("vector", (lambda v, k=k: v.tensor_scalar(xn1[k][:, 0:1], xn1[k][:, 0:1],
                                                        mask_col[:, :], None, OP.mult)),
              inc="V")
    omr_cols = [al.sb([128, 1], F32) for _ in range(KD)]
    omk_cols = [al.sb([128, 1], F32) for _ in range(KD)]
    for k in range(KD):
        s.add("vector", (lambda v, k=k: v.tensor_scalar(omr_cols[k][:, :], mur_cols[k][:, :],
                                                        -1.0, 1.0, OP.mult, OP.add)),
              inc="V")
        s.add("vector", (lambda v, k=k: v.tensor_scalar(omk_cols[k][:, :], muk_cols[k][:, :],
                                                        -1.0, 1.0, OP.mult, OP.add)),
              inc="V")
    mix_tmp = al.sb([128, TPC], F32)
    mix_out = al.sb([128, TPC], BF16)
    for k in range(KD):
        for (mu_c, om_c, dst) in ((mur_cols[k], omr_cols[k], xr_out),
                                  (muk_cols[k], omk_cols[k], xk_out)):
            st_war = [("ST", s.v("ST") - 16)] if s.v("ST") >= KD * 16 + 16 else []
            s.add("vector", (lambda v, k=k, mu_c=mu_c:
                             v.tensor_scalar(mix_tmp[:, :], xn1[k][:, 1:VW], mu_c[:, :],
                                             None, OP.mult)),
                  waits=st_war, inc="V")
            s.add("vector", (lambda v, k=k, om_c=om_c:
                             v.scalar_tensor_tensor(mix_out[:, :], xn1[k][:, 0:TPC], om_c[:, :],
                                                    mix_tmp[:, :], OP.mult, OP.add)),
                  inc="V")
            s.add("gpsimd", (lambda e, k=k, dst=dst: e.dma_start(out=dst[k * 128:(k + 1) * 128, :],
                                                                 in_=mix_out[:, :])),
                  waits=[("V", s.v("V"))], inc="ST", inc_amt=16)
            s.add("vector", lambda v: v.nop(), waits=[("ST", s.v("ST"))], inc="V")

    s.build(nc, [("ST", s.v("ST"))])
    return nc



def build_launchC():
    nc = bass.Bass("TRN2", target_bir_lowering=False)
    al = Alloc(nc)
    s = Sched()
    for nm in ("LD", "V", "S", "PE", "ST"):
        s.sem(nm)

    xr_hbm = nc.dram_tensor("xr", [D, TPC], BF16, kind="ExternalInput").ap()
    xk_hbm = nc.dram_tensor("xk", [D, TPC], BF16, kind="ExternalInput").ap()
    x1_hbm = nc.dram_tensor("x1", [D, TPC], F32, kind="ExternalInput").ap()
    wr_hbm = nc.dram_tensor("WrT", [D, D], F32, kind="ExternalInput").ap()
    wk_hbm = nc.dram_tensor("WkT", [D, D], F32, kind="ExternalInput").ap()
    wv_hbm = nc.dram_tensor("WvT", [D, D], F32, kind="ExternalInput").ap()
    ones_hbm = nc.dram_tensor("ones", [128, 1], F32, kind="ExternalInput").ap()
    onesr_hbm = nc.dram_tensor("onesr", [1, 128], F32, kind="ExternalInput").ap()

    x2_out = nc.dram_tensor("x2", [D, TPC], F32, kind="ExternalOutput").ap()
    v0_out = nc.dram_tensor("v0", [D, TPC], BF16, kind="ExternalOutput").ap()

    ones_col = al.sb([128, 1], F32)
    ones_row = al.sb([1, 128], F32)
    s.add("sync", lambda e: e.dma_start(out=ones_col[:, :], in_=ones_hbm), inc="LD", inc_amt=16)
    s.add("sync", lambda e: e.dma_start(out=ones_row[:, :], in_=onesr_hbm), inc="LD", inc_amt=16)

    xk_tiles = [al.sb([128, TPC], BF16) for _ in range(KD)]
    xr_tiles = [al.sb([128, TPC], BF16) for _ in range(KD)]
    ka_tiles = [al.sb([128, TPC], BF16) for _ in range(KD)]
    for k in range(KD):
        s.add("sync", (lambda e, k=k: e.dma_start(out=xk_tiles[k][:, :],
                                                  in_=xk_hbm[k * 128:(k + 1) * 128, :])),
              inc="LD", inc_amt=16)
        s.add("sync", (lambda e, k=k: e.dma_start(out=xr_tiles[k][:, :],
                                                  in_=xr_hbm[k * 128:(k + 1) * 128, :])),
              inc="LD", inc_amt=16)
    w_bf = [al.sb([128, D], BF16) for _ in range(KD)]
    w_stage = al.sb([128, D], F32)
    psc = [al.ps([128, 512], F32), al.ps([128, 512], F32)]
    se_c = [0, 0]
    ld_in = s.v("LD")
    s.add("vector", lambda v: v.nop(), waits=[("LD", ld_in)], inc="V")

    def gemm_to_tiles(src_tiles, out_tiles, act, post_square=False):
        gi = 0
        for mm in range(KD):
            for ci, (c0, w) in enumerate(t_chunks(TPC)):
                par = gi % 2
                gi += 1
                for k in range(KD):
                    waits = []
                    if k == 0:
                        waits = [("S", se_c[par])]
                        if mm == 0 and ci == 0:
                            waits = [("V", s.v("V")), ("S", s.v("S"))]
                    s.add("tensor", (lambda e, mm=mm, k=k, c0=c0, w=w, par=par:
                                     e.matmul(psc[par][:, :w], w_bf[k][:, mm * 128:(mm + 1) * 128],
                                              src_tiles[k][:, c0:c0 + w],
                                              start=(k == 0), stop=(k == KD - 1))),
                          waits=waits, inc="PE" if k == KD - 1 else None)
                s.add("scalar", (lambda e, mm=mm, c0=c0, w=w, par=par:
                                 e.activation(out_tiles[mm][:, c0:c0 + w], psc[par][:, :w], act)),
                      waits=[("PE", s.v("PE"))], inc="S")
                se_c[par] = s.v("S")
                if post_square:
                    s.add("scalar", (lambda e, mm=mm, c0=c0, w=w:
                                     e.activation(out_tiles[mm][:, c0:c0 + w],
                                                  out_tiles[mm][:, c0:c0 + w], AF.Square)),
                          inc="S")

    load_cast_weight(s, al, wk_hbm, KD, D, tiles=w_bf, stage=w_stage)
    gemm_to_tiles(xk_tiles, ka_tiles, AF.Relu, post_square=True)
    load_cast_weight(s, al, wv_hbm, KD, D, tiles=w_bf, stage=w_stage,
                     extra_war=[("PE", s.v("PE"))])
    gemm_to_tiles(ka_tiles, xk_tiles, AF.Identity)
    load_cast_weight(s, al, wr_hbm, KD, D, tiles=w_bf, stage=w_stage,
                     extra_war=[("PE", s.v("PE"))])
    gemm_to_tiles(xr_tiles, ka_tiles, AF.Sigmoid)

    x2_tiles = [al.sb([128, TPC], F32) for _ in range(KD)]
    x1_stage = al.sb([128, TPC], F32)
    for k in range(KD):
        s.add("sync", (lambda e, k=k: e.dma_start(out=x1_stage[:, :],
                                                  in_=x1_hbm[k * 128:(k + 1) * 128, :])),
              waits=[("V", s.v("V"))], inc="LD", inc_amt=16)
        s.add("vector", (lambda v, k=k: v.tensor_mul(x2_tiles[k][:, :], ka_tiles[k][:, :],
                                                     xk_tiles[k][:, :])),
              waits=[("S", s.v("S")), ("LD", s.v("LD"))], inc="V")
        s.add("vector", (lambda v, k=k: v.tensor_add(x2_tiles[k][:, :], x2_tiles[k][:, :],
                                                     x1_stage[:, :])),
              inc="V")
        s.add("gpsimd", (lambda e, k=k: e.dma_start(out=x2_out[k * 128:(k + 1) * 128, :],
                                                    in_=x2_tiles[k][:, :])),
              waits=[("V", s.v("V"))], inc="ST", inc_amt=16)

    v0_tiles = xr_tiles
    layernorm_fm(s, al, x2_tiles, TPC, v0_tiles, ones_col, ones_row)
    for k in range(KD):
        s.add("gpsimd", (lambda e, k=k: e.dma_start(out=v0_out[k * 128:(k + 1) * 128, :],
                                                    in_=v0_tiles[k][:, :])),
              waits=[("V", s.v("V"))], inc="ST", inc_amt=16)

    s.build(nc, [("ST", s.v("ST"))])
    return nc



def build_launchD():
    nc = bass.Bass("TRN2", target_bir_lowering=False)
    al = Alloc(nc)
    s = Sched()
    for nm in ("LD", "V", "S", "PE", "ST"):
        s.sem(nm)

    v_hbm = nc.dram_tensor("v_full", [D, T], BF16, kind="ExternalInput").ap()
    inw_hbm = nc.dram_tensor("in_wT_sl", [D, 1024], F32, kind="ExternalInput").ap()
    cw_hbm = nc.dram_tensor("conv_w_sl", [DSL, D_CONV], F32, kind="ExternalInput").ap()
    cb_hbm = nc.dram_tensor("conv_b_sl", [DSL], F32, kind="ExternalInput").ap()
    xpw_hbm = nc.dram_tensor("xp_wT_sl", [DSL, 96], F32, kind="ExternalInput").ap()

    u_out = nc.dram_tensor("u", [DSL, T], BF16, kind="ExternalOutput").ap()
    z_out = nc.dram_tensor("z", [DSL, T], BF16, kind="ExternalOutput").ap()
    xdbl_out = nc.dram_tensor("xdbl_p", [96, T], F32, kind="ExternalOutput").ap()

    inw_bf, _ = load_cast_weight(s, al, inw_hbm, KD, 1024)
    cw_tiles = [al.sb([128, D_CONV], F32) for _ in range(4)]
    s.add("sync", lambda e: e.nop(), waits=[("V", s.v("V"))])
    cb_cols = load_bias_cols(s, al, cb_hbm, 4)
    for k in range(4):
        s.add("sync", (lambda e, k=k: e.dma_start(out=cw_tiles[k][:, :],
                                                  in_=cw_hbm[k * 128:(k + 1) * 128, :])),
              inc="LD", inc_amt=16)
    xpw_bf, _ = load_cast_weight(s, al, xpw_hbm, 4, 96)
    ld_const = s.v("LD")
    s.add("scalar", lambda e: e.nop(), waits=[("LD", ld_const)], inc="S")
    s.add("vector", lambda v: v.nop(), waits=[("LD", ld_const)], inc="V")

    TC = 512
    v_tiles = [al.sb([128, TC], BF16) for _ in range(KD)]
    ps = al.ps([128, 512], F32)
    ps96 = al.ps([96, 512], F32)
    xc_ext = [al.sb([128, 3 + TC], F32) for _ in range(4)]
    u_bf = [al.sb([128, TC], BF16) for _ in range(4)]
    z_bf = al.sb([128, TC], BF16)
    conv_acc = al.sb([128, TC], F32)
    xdbl_stage = al.sb([96, TC], F32)
    for k in range(4):
        s.add("vector", (lambda v, k=k: v.memset(xc_ext[k][:, 0:3], 0.0)), inc="V")

    for ci, (c0, w) in enumerate(t_chunks(T, TC)):
        for k in range(KD):
            war = []
            if k == 0:
                war = [("PE", s.v("PE")), ("V", s.v("V"))] if ci > 0 else [("V", s.v("V"))]
            s.add("sync", (lambda e, k=k, c0=c0, w=w:
                           e.dma_start(out=v_tiles[k][:, :w], in_=v_hbm[k * 128:(k + 1) * 128, c0:c0 + w])),
                  waits=war, inc="LD", inc_amt=16)
        ld_v = s.v("LD")
        for m in range(8):
            for k in range(KD):
                waits = []
                if k == 0:
                    waits = [("S", s.v("S")), ("LD", ld_v)]
                    if m == 0 and ci == 0:
                        waits += [("V", s.v("V"))]
                s.add("tensor", (lambda e, m=m, k=k, w=w:
                                 e.matmul(ps[:, :w], inw_bf[k][:, m * 128:(m + 1) * 128],
                                          v_tiles[k][:, :w], start=(k == 0), stop=(k == KD - 1))),
                      waits=waits, inc="PE" if k == KD - 1 else None)
            if m < 4:
                s.add("scalar", (lambda e, m=m, w=w:
                                 e.activation(xc_ext[m][:, 3:3 + w], ps[:, :w], AF.Identity)),
                      waits=[("PE", s.v("PE")), ("V", s.v("V"))], inc="S")
            else:
                s.add("scalar", (lambda e, w=w:
                                 e.activation(z_bf[:, :w], ps[:, :w], AF.Identity)),
                      waits=[("PE", s.v("PE")), ("ST", s.v("ST"))], inc="S")
                s.add("gpsimd", (lambda e, m=m, c0=c0, w=w:
                                 e.dma_start(out=z_out[(m - 4) * 128:(m - 3) * 128, c0:c0 + w],
                                             in_=z_bf[:, :w])),
                      waits=[("S", s.v("S"))], inc="ST", inc_amt=16)
        for dk in range(4):
            s.add("vector", (lambda v, dk=dk, w=w:
                             v.tensor_scalar(conv_acc[:, :w], xc_ext[dk][:, 0:w],
                                             cw_tiles[dk][:, 0:1], None, OP.mult)),
                  waits=[("S", s.v("S"))], inc="V")
            for kk in range(1, 4):
                s.add("vector", (lambda v, dk=dk, kk=kk, w=w:
                                 v.scalar_tensor_tensor(conv_acc[:, :w], xc_ext[dk][:, kk:kk + w],
                                                        cw_tiles[dk][:, kk:kk + 1],
                                                        conv_acc[:, :w], OP.mult, OP.add)),
                      inc="V")
            s.add("scalar", (lambda e, dk=dk, w=w:
                             e.activation(u_bf[dk][:, :w], conv_acc[:, :w], AF.Silu,
                                          bias=cb_cols[dk][:, :])),
                  waits=[("V", s.v("V")), ("PE", s.v("PE"))], inc="S")
            s.add("gpsimd", (lambda e, dk=dk, c0=c0, w=w:
                             e.dma_start(out=u_out[dk * 128:(dk + 1) * 128, c0:c0 + w],
                                         in_=u_bf[dk][:, :w])),
                  waits=[("S", s.v("S"))], inc="ST", inc_amt=16)
            s.add("vector", (lambda v, dk=dk, w=w:
                             v.tensor_copy(xc_ext[dk][:, 0:3], xc_ext[dk][:, w:w + 3])),
                  waits=[("S", s.v("S"))], inc="V")
        for k in range(4):
            s.add("tensor", (lambda e, k=k, w=w:
                             e.matmul(ps96[:, :w], xpw_bf[k][:, :], u_bf[k][:, :w],
                                      start=(k == 0), stop=(k == 3))),
                  waits=[("S", s.v("S"))] if k == 0 else [], inc="PE" if k == 3 else None)
        s.add("scalar", (lambda e, w=w: e.activation(xdbl_stage[:, :w], ps96[:, :w], AF.Identity)),
              waits=[("PE", s.v("PE")), ("ST", s.v("ST"))], inc="S")
        s.add("gpsimd", (lambda e, c0=c0, w=w:
                         e.dma_start(out=xdbl_out[:, c0:c0 + w], in_=xdbl_stage[:, :w])),
              waits=[("S", s.v("S"))], inc="ST", inc_amt=16)

    s.build(nc, [("ST", s.v("ST"))])
    return nc



def build_launchE():
    nc = bass.Bass("TRN2", target_bir_lowering=False)
    al = Alloc(nc)
    s = Sched()
    for nm in ("LD", "V", "S", "PE", "ST", "GP", "RD"):
        s.sem(nm)

    u_hbm = nc.dram_tensor("u", [DSL, T], BF16, kind="ExternalInput").ap()
    z_hbm = nc.dram_tensor("z", [DSL, T], BF16, kind="ExternalInput").ap()
    xdbl_hbm = nc.dram_tensor("xdbl_all", [4, 96, T], F32, kind="ExternalInput").ap()
    dtw_hbm = nc.dram_tensor("dt_wT_sl", [DT_RANK, DSL], F32, kind="ExternalInput").ap()
    dtb_hbm = nc.dram_tensor("dt_b_sl", [DSL], F32, kind="ExternalInput").ap()
    alog_hbm = nc.dram_tensor("A_log_sl", [DSL, D_STATE], F32, kind="ExternalInput").ap()
    dsk_hbm = nc.dram_tensor("Dskip_sl", [DSL], F32, kind="ExternalInput").ap()
    onesr_hbm = nc.dram_tensor("onesr", [1, 128], F32, kind="ExternalInput").ap()
    selbc_hbm = nc.dram_tensor("selBC", [32 * 96, 128], F32, kind="ExternalInput").ap()

    y_out = nc.dram_tensor("y", [DSL, T], BF16, kind="ExternalOutput").ap()

    ones_row = al.sb([1, 128], F32)
    s.add("sync", lambda e: e.dma_start(out=ones_row[:, :], in_=onesr_hbm), inc="LD", inc_amt=16)
    dtb_cols = load_bias_cols(s, al, dtb_hbm, 4)
    dsk_cols = load_bias_cols(s, al, dsk_hbm, 4)
    acol = []
    for k in range(4):
        t = al.sb([128, D_STATE], F32)
        s.add("sync", (lambda e, k=k, t=t: e.dma_start(out=t[:, :],
                                                       in_=alog_hbm[k * 128:(k + 1) * 128, :])),
              waits=[("V", s.v("V"))] if k > 0 else [], inc="LD", inc_amt=16)
        s.add("scalar", (lambda e, t=t: e.activation(t[:, :], t[:, :], AF.Exp)),
              waits=[("LD", s.v("LD"))], inc="S")
        s.add("vector", (lambda v, t=t: v.tensor_scalar_mul(t[:, :], t[:, :], -1.0)),
              waits=[("S", s.v("S"))], inc="V")
        acol.append(t)
    dtw_stage = al.sb([64, DSL], F32)
    dtw_bf = al.sb([64, DSL], BF16)
    s.add("sync", lambda e: e.dma_start(out=dtw_stage[:, :], in_=dtw_hbm),
          waits=[("V", s.v("V"))], inc="LD", inc_amt=16)
    s.add("vector", lambda v: v.tensor_copy(dtw_bf[:, :], dtw_stage[:, :]),
          waits=[("LD", s.v("LD"))], inc="V")
    selbc_stage = al.sb([96, 128], F32)
    for i in range(32):
        s.add("sync", (lambda e, i=i: e.dma_start(out=selbc_stage[:, :],
                                                  in_=selbc_hbm[i * 96:(i + 1) * 96, :])),
              waits=[("V", s.v("V"))], inc="LD", inc_amt=16)
        s.add("vector", (lambda v, i=i: v.tensor_copy(selbc[i][:, :], selbc_stage[:, :])),
              waits=[("LD", s.v("LD"))], inc="V")

    xdbl = al.sb([96, T], BF16)
    sp_tmp = al.sb([128, 2048], F32)
    xstage = sp_tmp
    for ci, (c0, w) in enumerate(t_chunks(T, 2048)):
        for p in range(4):
            s.add("sync", (lambda e, p=p, c0=c0, w=w:
                           e.dma_start(out=xstage[0:96, :w], in_=xdbl_hbm[p, :, c0:c0 + w])),
                  waits=[("V", s.v("V"))], inc="LD", inc_amt=16)
            if p == 0:
                s.add("vector", (lambda v, c0=c0, w=w: v.tensor_copy(xdbl[:, c0:c0 + w], xstage[0:96, :w])),
                      waits=[("LD", s.v("LD"))], inc="V")
            else:
                s.add("vector", (lambda v, c0=c0, w=w:
                                 v.tensor_add(xdbl[:, c0:c0 + w], xdbl[:, c0:c0 + w], xstage[0:96, :w])),
                      waits=[("LD", s.v("LD"))], inc="V")

    TSC = 2048
    u_tiles = [al.sb([128, TSC], BF16) for _ in range(4)]
    dt_tiles = [al.sb([128, TSC], BF16) for _ in range(4)]
    udt_tiles = [al.sb([128, TSC], BF16) for _ in range(4)]
    z_tiles = [al.sb([128, TSC], BF16) for _ in range(4)]
    a_sb = [al.sb([128, TSC], F32) for _ in range(2)]
    b_sb = al.sb([128, TSC], BF16)
    h_sb = [al.sb([128, TSC], F32) for _ in range(2)]
    zg_sb = [al.sb([128, TSC], BF16) for _ in range(2)]
    y_tiles = [al.sb([128, TSC], F32) for _ in range(4)]
    selbc = [al.sb([96, 128], BF16) for _ in range(32)]
    b_bc = [al.sb([128, TSC], BF16) for _ in range(2)]
    c_bc = [al.sb([128, TSC], BF16) for _ in range(2)]
    slab = al.sb([128, 64], F32)
    y_stage = al.sb([128, TSC], BF16)
    ps = al.ps([128, 512], F32)
    bc_ps = al.ps([128, 512], F32)
    s.add("vector", lambda v: v.memset(slab[:, :], 0.0), inc="V")

    for ci, (c0, w) in enumerate(t_chunks(T, TSC)):
        for k in range(4):
            war = ([("V", s.v("V")), ("S", s.v("S"))] if ci > 0 else [("V", s.v("V"))]) \
                if k == 0 else []
            s.add("sync", (lambda e, k=k, c0=c0, w=w:
                           e.dma_start(out=u_tiles[k][:, :w], in_=u_hbm[k * 128:(k + 1) * 128, c0:c0 + w])),
                  waits=war, inc="LD", inc_amt=16)
            s.add("sync", (lambda e, k=k, c0=c0, w=w:
                           e.dma_start(out=z_tiles[k][:, :w], in_=z_hbm[k * 128:(k + 1) * 128, c0:c0 + w])),
                  inc="LD", inc_amt=16)
        ld_uz = s.v("LD")
        for dk in range(4):
            for c1, w1 in t_chunks(w):
                s.add("tensor", (lambda e, dk=dk, c0=c0, c1=c1, w1=w1:
                                 e.matmul(ps[:, :w1], dtw_bf[:, dk * 128:(dk + 1) * 128],
                                          xdbl[0:64, c0 + c1:c0 + c1 + w1], start=True, stop=True)),
                      waits=[("S", s.v("S")), ("V", s.v("V"))] if (dk == 0 and c1 == 0) else
                            [("S", s.v("S"))],
                      inc="PE")
                s.add("scalar", (lambda e, dk=dk, c1=c1, w1=w1:
                                 e.activation(dt_tiles[dk][:, c1:c1 + w1], ps[:, :w1], AF.Identity,
                                              bias=dtb_cols[dk][:, :])),
                      waits=[("PE", s.v("PE"))], inc="S")
            s.add("scalar", (lambda e, dk=dk, w=w:
                             e.activation(sp_tmp[:, :w], dt_tiles[dk][:, :w], AF.Abs)),
                  inc="S")
            s.add("scalar", (lambda e, dk=dk, w=w:
                             e.activation(sp_tmp[:, :w], sp_tmp[:, :w], AF.Exp, scale=-1.0)),
                  inc="S")
            s.add("scalar", (lambda e, dk=dk, w=w:
                             e.activation(sp_tmp[:, :w], sp_tmp[:, :w], AF.Ln, bias=1.0)),
                  inc="S")
            s.add("vector", (lambda v, dk=dk, w=w:
                             v.tensor_scalar_max(dt_tiles[dk][:, :w], dt_tiles[dk][:, :w], 0.0)),
                  waits=[("S", s.v("S"))], inc="V")
            s.add("vector", (lambda v, dk=dk, w=w:
                             v.tensor_add(dt_tiles[dk][:, :w], dt_tiles[dk][:, :w], sp_tmp[:, :w])),
                  inc="V")
            s.add("vector", (lambda v, dk=dk, w=w:
                             v.tensor_mul(udt_tiles[dk][:, :w], dt_tiles[dk][:, :w],
                                          u_tiles[dk][:, :w])),
                  waits=[("LD", ld_uz)], inc="V")
            s.add("scalar", lambda e: e.nop(), waits=[("V", s.v("V"))], inc="S")
        for dk in range(4):
            s.add("vector", (lambda v, dk=dk, w=w: v.memset(y_tiles[dk][:, :w], 0.0)), inc="V")
        dt_ready_v = s.v("V")

        v_tts = {}
        gp_z = {}
        n_end = {}
        pend = None
        chunk_start_gp = s.v("GP")

        for n in range(D_STATE):
            par = n % 2
            if n >= 2:
                vv, gg = n_end[n - 2]
                bc_war = [("V", vv), ("GP", gg)]
            else:
                bc_war = [("V", dt_ready_v), ("GP", chunk_start_gp)]
            for c1, w1 in t_chunks(w):
                s.add("tensor", (lambda e, n=n, c0=c0, c1=c1, w1=w1:
                                 e.matmul(bc_ps[:, :w1], selbc[n][:, :],
                                          xdbl[0:96, c0 + c1:c0 + c1 + w1],
                                          start=True, stop=True)),
                      waits=[("S", s.v("S"))] + bc_war, inc="PE")
                s.add("scalar", (lambda e, par=par, c1=c1, w1=w1:
                                 e.copy(b_bc[par][:, c1:c1 + w1], bc_ps[:, :w1])),
                      waits=[("PE", s.v("PE"))] + bc_war, inc="S")
                s.add("tensor", (lambda e, n=n, c0=c0, c1=c1, w1=w1:
                                 e.matmul(bc_ps[:, :w1], selbc[16 + n][:, :],
                                          xdbl[0:96, c0 + c1:c0 + c1 + w1],
                                          start=True, stop=True)),
                      waits=[("S", s.v("S"))], inc="PE")
                s.add("scalar", (lambda e, par=par, c1=c1, w1=w1:
                                 e.copy(c_bc[par][:, c1:c1 + w1], bc_ps[:, :w1])),
                      waits=[("PE", s.v("PE"))], inc="S")
            for dk in range(4):
                it = n * 4 + dk
                apar = it % 2
                hpar = it % 2
                if pend is not None:
                    pdk, pzp, pgp = pend
                    s.add("vector", (lambda v, pdk=pdk, pzp=pzp, w=w:
                                     v.tensor_add(y_tiles[pdk][:, :w], y_tiles[pdk][:, :w],
                                                  zg_sb[pzp][:, :w])),
                          waits=[("GP", pgp)], inc="V")
                a_war = [("V", v_tts[it - 2])] if it >= 2 else [("V", dt_ready_v)]
                s.add("scalar", (lambda e, dk=dk, n=n, apar=apar, w=w:
                                 e.activation(a_sb[apar][:, :w], dt_tiles[dk][:, :w], AF.Exp,
                                              scale=acol[dk][:, n:n + 1])),
                      waits=a_war, inc="S")
                s.add("vector", (lambda v, dk=dk, par=par, w=w:
                                 v.tensor_mul(b_sb[:, :w], udt_tiles[dk][:, :w], b_bc[par][:, :w])),
                      waits=[("S", s.v("S"))], inc="V")
                tts_war = [("GP", gp_z[it - 2])] if it >= 2 else [("GP", chunk_start_gp)]
                s.add("vector", (lambda v, apar=apar, hpar=hpar, sidx=it, w=w:
                                 v.tensor_tensor_scan(h_sb[hpar][:, :w], a_sb[apar][:, :w], b_sb[:, :w],
                                                      slab[:, sidx:sidx + 1], OP.mult, OP.add)),
                      waits=tts_war, inc="V")
                v_tts[it] = s.v("V")
                s.add("vector", (lambda v, hpar=hpar, sidx=it, w=w:
                                 v.tensor_copy(slab[:, sidx:sidx + 1], h_sb[hpar][:, w - 1:w])),
                      waits=[("V", s.v("V"))], inc="V")
                s.add("gpsimd", (lambda g, par=par, hpar=hpar, zp=it % 2, w=w:
                                 g.tensor_mul(zg_sb[zp][:, :w], h_sb[hpar][:, :w], c_bc[par][:, :w])),
                      waits=[("V", s.v("V"))], inc="GP")
                gp_z[it] = s.v("GP")
                pend = (dk, it % 2, gp_z[it])
            n_end[n] = (s.v("V"), s.v("GP"))
        if pend is not None:
            pdk, pzp, pgp = pend
            s.add("vector", (lambda v, pdk=pdk, pzp=pzp, w=w:
                             v.tensor_add(y_tiles[pdk][:, :w], y_tiles[pdk][:, :w],
                                          zg_sb[pzp][:, :w])),
                  waits=[("GP", pgp)], inc="V")
            pend = None

        for dk in range(4):
            s.add("vector", (lambda v, dk=dk, w=w:
                             v.scalar_tensor_tensor(y_tiles[dk][:, :w], u_tiles[dk][:, :w],
                                                    dsk_cols[dk][:, :], y_tiles[dk][:, :w],
                                                    OP.mult, OP.add)),
                  inc="V")
            s.add("scalar", (lambda e, dk=dk, w=w:
                             e.activation(sp_tmp[:, :w], z_tiles[dk][:, :w], AF.Silu)),
                  waits=[("V", s.v("V"))], inc="S")
            st_war = [("ST", s.v("ST"))] if (ci * 4 + dk) >= 1 else []
            s.add("vector", (lambda v, dk=dk, w=w:
                             v.tensor_mul(y_stage[:, :w], y_tiles[dk][:, :w], sp_tmp[:, :w])),
                  waits=[("S", s.v("S"))] + st_war, inc="V")
            s.add("gpsimd", (lambda e, dk=dk, c0=c0, w=w:
                             e.dma_start(out=y_out[dk * 128:(dk + 1) * 128, c0:c0 + w],
                                         in_=y_stage[:, :w])),
                  waits=[("V", s.v("V"))], inc="ST", inc_amt=16)
            s.add("scalar", lambda e: e.nop(), waits=[("V", s.v("V"))], inc="S")

    s.build(nc, [("ST", s.v("ST"))])
    return nc



def build_launchF(vin_f32):
    nc = bass.Bass("TRN2", target_bir_lowering=False)
    al = Alloc(nc)
    s = Sched()
    for nm in ("LD", "V", "S", "PE", "ST"):
        s.sem(nm)

    y_hbm = nc.dram_tensor("y_b", [D_INNER, TPC], BF16, kind="ExternalInput").ap()
    ow_hbm = nc.dram_tensor("out_wT", [D_INNER, D], F32, kind="ExternalInput").ap()
    lnw_hbm = nc.dram_tensor("ln_w", [D], F32, kind="ExternalInput").ap()
    lnb_hbm = nc.dram_tensor("ln_b", [D], F32, kind="ExternalInput").ap()
    vin_hbm = nc.dram_tensor("v_in", [D, TPC], F32 if vin_f32 else BF16,
                             kind="ExternalInput").ap()
    ones_hbm = nc.dram_tensor("ones", [128, 1], F32, kind="ExternalInput").ap()
    onesr_hbm = nc.dram_tensor("onesr", [1, 128], F32, kind="ExternalInput").ap()

    vf_out = nc.dram_tensor("v_f", [D, TPC], F32, kind="ExternalOutput").ap()
    vb_out = nc.dram_tensor("v_b", [D, TPC], BF16, kind="ExternalOutput").ap()

    ones_col = al.sb([128, 1], F32)
    ones_row = al.sb([1, 128], F32)
    s.add("sync", lambda e: e.dma_start(out=ones_col[:, :], in_=ones_hbm), inc="LD", inc_amt=16)
    s.add("sync", lambda e: e.dma_start(out=ones_row[:, :], in_=onesr_hbm), inc="LD", inc_amt=16)
    g_cols = load_bias_cols(s, al, lnw_hbm, KD)
    b_cols = load_bias_cols(s, al, lnb_hbm, KD)
    ow_bf, _ = load_cast_weight(s, al, ow_hbm, 16, D)
    ld0 = s.v("LD")
    s.add("scalar", lambda e: e.nop(), waits=[("LD", ld0)], inc="S")

    y_ch = [al.sb([128, 512], BF16) for _ in range(16)]
    h_ch = [al.sb([128, 512], F32) for _ in range(KD)]
    hn_ch = [al.sb([128, 512], F32) for _ in range(KD)]
    vin_ch = [al.sb([128, 512], F32 if vin_f32 else BF16) for _ in range(KD)]
    vout_ch = [al.sb([128, 512], F32) for _ in range(KD)]
    vb_st = al.sb([128, 512], BF16)
    psf = [al.ps([128, 512], F32), al.ps([128, 512], F32)]
    se_f = [0, 0]
    lbufs = ln_bufs(al)

    for ci, (c0, w) in enumerate(t_chunks(TPC)):
        for k in range(16):
            war = []
            if k == 0:
                war = [("PE", s.v("PE"))] if ci > 0 else [("S", s.v("S"))]
            s.add("sync", (lambda e, k=k, c0=c0, w=w:
                           e.dma_start(out=y_ch[k][:, :w], in_=y_hbm[k * 128:(k + 1) * 128, c0:c0 + w])),
                  waits=war, inc="LD", inc_amt=16)
        for k in range(KD):
            war = []
            if k == 0:
                war = [("V", s.v("V"))] if ci > 0 else [("S", s.v("S"))]
            s.add("sync", (lambda e, k=k, c0=c0, w=w:
                           e.dma_start(out=vin_ch[k][:, :w], in_=vin_hbm[k * 128:(k + 1) * 128, c0:c0 + w])),
                  waits=war, inc="LD", inc_amt=16)
        ld_ch = s.v("LD")
        for m in range(KD):
            par = m % 2
            for k in range(16):
                waits = []
                if k == 0:
                    waits = [("S", se_f[par]), ("LD", ld_ch)]
                    if m == 0 and ci == 0:
                        waits += [("V", s.v("V")), ("S", s.v("S"))]
                s.add("tensor", (lambda e, m=m, k=k, w=w, par=par:
                                 e.matmul(psf[par][:, :w], ow_bf[k][:, m * 128:(m + 1) * 128],
                                          y_ch[k][:, :w], start=(k == 0), stop=(k == 15))),
                      waits=waits, inc="PE" if k == 15 else None)
            s.add("scalar", (lambda e, m=m, w=w, par=par:
                             e.activation(h_ch[m][:, :w], psf[par][:, :w], AF.Identity)),
                  waits=[("PE", s.v("PE")), ("V", s.v("V"))], inc="S")
            se_f[par] = s.v("S")
        layernorm_fm(s, al, h_ch, w, hn_ch, ones_col, ones_row,
                     gamma_col=g_cols, beta_col=b_cols, bufs=lbufs)
        for k in range(KD):
            s.add("vector", (lambda v, k=k, w=w:
                             v.tensor_add(vout_ch[k][:, :w], hn_ch[k][:, :w], vin_ch[k][:, :w])),
                  inc="V")
            s.add("gpsimd", (lambda e, k=k, c0=c0, w=w:
                             e.dma_start(out=vf_out[k * 128:(k + 1) * 128, c0:c0 + w],
                                         in_=vout_ch[k][:, :w])),
                  waits=[("V", s.v("V"))], inc="ST", inc_amt=16)
            st_war = [("ST", s.v("ST"))] if (ci > 0 or k > 0) else []
            s.add("vector", (lambda v, k=k, w=w:
                             v.tensor_copy(vb_st[:, :w], vout_ch[k][:, :w])),
                  waits=st_war, inc="V")
            s.add("gpsimd", (lambda e, k=k, c0=c0, w=w:
                             e.dma_start(out=vb_out[k * 128:(k + 1) * 128, c0:c0 + w],
                                         in_=vb_st[:, :w])),
                  waits=[("V", s.v("V"))], inc="ST", inc_amt=16)

    s.build(nc, [("ST", s.v("ST"))])
    return nc



def build_launchG():
    nc = bass.Bass("TRN2", target_bir_lowering=False)
    al = Alloc(nc)
    s = Sched()
    for nm in ("LD", "V", "S", "PE", "ST", "RD", "AC", "GD"):
        s.sem(nm)

    x2_hbm = nc.dram_tensor("x2", [D, TPC], F32, kind="ExternalInput").ap()
    v2_hbm = nc.dram_tensor("v2", [D, TPC], F32, kind="ExternalInput").ap()
    nsc_hbm = nc.dram_tensor("norm_scale", [D], F32, kind="ExternalInput").ap()
    rt_hbm = nc.dram_tensor("routerT", [D, N_EXP], F32, kind="ExternalInput").ap()
    wup_hbm = nc.dram_tensor("w_upT", [N_EXP, D, 4096], F32, kind="ExternalInput").ap()
    wdn_hbm = nc.dram_tensor("w_downT", [N_EXP, H_MOE, D], F32, kind="ExternalInput").ap()
    eye_hbm = nc.dram_tensor("eye128", [128, 128], F32, kind="ExternalInput").ap()
    selg_hbm = nc.dram_tensor("selG", [N_EXP * N_EXP, 128], F32, kind="ExternalInput").ap()
    iota_hbm = nc.dram_tensor("iota8", [128, N_EXP], F32, kind="ExternalInput").ap()
    revi_hbm = nc.dram_tensor("revi8", [128, N_EXP], F32, kind="ExternalInput").ap()
    ones_hbm = nc.dram_tensor("ones", [128, 1], F32, kind="ExternalInput").ap()
    onesr_hbm = nc.dram_tensor("onesr", [1, 128], F32, kind="ExternalInput").ap()

    out_hbm = nc.dram_tensor("out_fm", [D, TPC], F32, kind="ExternalOutput").ap()
    acc_hbm = nc.dram_tensor("acc_scratch", [D, TPC], F32, kind="Internal").ap()

    ones_col = al.sb([128, 1], F32)
    ones_row = al.sb([1, 128], F32)
    eye_sb = al.sb([128, 128], F32)
    iota_sb = al.sb([128, N_EXP], F32)
    revi_sb = al.sb([128, N_EXP], F32)
    s.add("sync", lambda e: e.dma_start(out=ones_col[:, :], in_=ones_hbm), inc="LD", inc_amt=16)
    s.add("sync", lambda e: e.dma_start(out=ones_row[:, :], in_=onesr_hbm), inc="LD", inc_amt=16)
    s.add("sync", lambda e: e.dma_start(out=eye_sb[:, :], in_=eye_hbm), inc="LD", inc_amt=16)
    s.add("sync", lambda e: e.dma_start(out=iota_sb[:, :], in_=iota_hbm), inc="LD", inc_amt=16)
    s.add("sync", lambda e: e.dma_start(out=revi_sb[:, :], in_=revi_hbm), inc="LD", inc_amt=16)
    nsc_cols = load_bias_cols(s, al, nsc_hbm, KD)
    rt_bf, _ = load_cast_weight(s, al, rt_hbm, KD, N_EXP)
    ld0 = s.v("LD")
    s.add("scalar", lambda e: e.nop(), waits=[("LD", ld0)], inc="S")
    s.add("vector", lambda v: v.nop(), waits=[("LD", ld0)], inc="V")

    xn3 = [al.sb([128, TPC], BF16) for _ in range(KD)]
    x3_ch = [al.sb([128, 512], F32) for _ in range(KD)]
    _pool = al.sb([128, 8 * 512], F32)
    ln3_ch = [_pool[:, i * 512:(i + 1) * 512] for i in range(KD)]
    stage_a = al.sb([128, 512], F32)
    lbufs = ln_bufs(al)
    for ci, (c0, w) in enumerate(t_chunks(TPC)):
        for k in range(KD):
            war = [("V", s.v("V"))]
            if ci > 0 and k == 0:
                war += [("ST", s.v("ST"))]
            s.add("sync", (lambda e, k=k, c0=c0, w=w:
                           e.dma_start(out=x3_ch[k][:, :w], in_=x2_hbm[k * 128:(k + 1) * 128, c0:c0 + w])),
                  waits=war, inc="LD", inc_amt=16)
            s.add("sync", (lambda e, k=k, c0=c0, w=w:
                           e.dma_start(out=stage_a[:, :w], in_=v2_hbm[k * 128:(k + 1) * 128, c0:c0 + w])),
                  waits=[("V", s.v("V"))], inc="LD", inc_amt=16)
            s.add("vector", (lambda v, k=k, w=w: v.tensor_add(x3_ch[k][:, :w], x3_ch[k][:, :w],
                                                              stage_a[:, :w])),
                  waits=[("LD", s.v("LD"))], inc="V")
        layernorm_fm(s, al, x3_ch, w, ln3_ch, ones_col, ones_row, bufs=lbufs)
        xn3_views = [xn3[k][:, c0:c0 + w] for k in range(KD)]
        layernorm_fm(s, al, ln3_ch, w, xn3_views, ones_col, ones_row, rms=True, eps=EPS_RMS,
                     gamma_col=nsc_cols, bufs=lbufs)
        for k in range(KD):
            s.add("vector", (lambda v, k=k, w=w: v.tensor_add(x3_ch[k][:, :w], x3_ch[k][:, :w],
                                                              ln3_ch[k][:, :w])),
                  inc="V")
            s.add("gpsimd", (lambda e, k=k, c0=c0, w=w:
                             e.dma_start(out=acc_hbm[k * 128:(k + 1) * 128, c0:c0 + w],
                                         in_=x3_ch[k][:, :w])),
                  waits=[("V", s.v("V"))], inc="ST", inc_amt=16)

    ps8 = al.ps([128, N_EXP], F32)
    sc_tm = al.sb([128, 16 * N_EXP], F32)
    w8_tm = al.sb([128, 16 * N_EXP], F32)
    t8a = al.sb([128, N_EXP], F32)
    t8b = al.sb([128, N_EXP], F32)
    m1c = al.sb([128, 1], F32)
    m2c = al.sb([128, 1], F32)
    i1c = al.sb([128, 1], F32)
    w1c = al.sb([128, 1], F32)
    w2c = al.sb([128, 1], F32)
    tps = al.ps([N_EXP, 128], F32)
    w8e = al.sb([N_EXP, TPC], F32)
    we_bc = al.sb([128, TPC], BF16)
    selg = [al.sb([N_EXP, 128], F32) for _ in range(N_EXP)]
    bc_ps = al.ps([128, 512], F32)
    for i in range(N_EXP):
        s.add("sync", (lambda e, i=i: e.dma_start(out=selg[i][:, :],
                                                  in_=selg_hbm[i * N_EXP:(i + 1) * N_EXP, :])),
              waits=[("V", s.v("V"))], inc="LD", inc_amt=16)

    for j in range(16):
        for k in range(KD):
            s.add("tensor", (lambda e, j=j, k=k:
                             e.matmul(ps8[:, :], xn3[k][:, j * 128:(j + 1) * 128], rt_bf[k][:, :],
                                      start=(k == 0), stop=(k == KD - 1))),
                  waits=[("V", s.v("V")), ("S", s.v("S"))] if k == 0 else [],
                  inc="PE" if k == KD - 1 else None)
        s.add("scalar", (lambda e, j=j: e.copy(sc_tm[:, j * N_EXP:(j + 1) * N_EXP], ps8[:, :])),
              waits=[("PE", s.v("PE"))], inc="S")
        sl = slice(j * N_EXP, (j + 1) * N_EXP)
        def vop(emit, extra=()):
            s.add("vector", emit, waits=[("V", s.v("V"))] + list(extra), inc="V")
        vop(lambda v, sl=sl: v.reduce_max(m1c[:, :], sc_tm[:, sl], axis=AX.X),
            extra=[("S", s.v("S"))])
        vop(lambda v, sl=sl: v.tensor_scalar(t8a[:, :], sc_tm[:, sl], m1c[:, :], None, OP.is_equal))
        vop(lambda v: v.tensor_mul(t8a[:, :], t8a[:, :], revi_sb[:, :]))
        vop(lambda v: v.reduce_max(i1c[:, :], t8a[:, :], axis=AX.X))
        vop(lambda v: v.tensor_scalar(i1c[:, :], i1c[:, :], -1.0, 7.0, OP.mult, OP.add))
        vop(lambda v: v.tensor_scalar(t8a[:, :], iota_sb[:, :], i1c[:, :], None, OP.is_equal))
        vop(lambda v, sl=sl: v.scalar_tensor_tensor(t8b[:, :], t8a[:, :], -1e30,
                                                    sc_tm[:, sl], OP.mult, OP.add))
        vop(lambda v: v.reduce_max(m2c[:, :], t8b[:, :], axis=AX.X))
        vop(lambda v: v.tensor_sub(w1c[:, :], m1c[:, :], m2c[:, :]))
        s.add("scalar", lambda e: e.activation(w1c[:, :], w1c[:, :], AF.Sigmoid),
              waits=[("V", s.v("V"))], inc="S")
        s.add("vector", lambda v: v.tensor_scalar(w2c[:, :], w1c[:, :], -1.0, 1.0, OP.mult, OP.add),
              waits=[("S", s.v("S")), ("V", s.v("V"))], inc="V")
        vop(lambda v: v.tensor_scalar(t8b[:, :], t8b[:, :], m2c[:, :], None, OP.is_equal))
        vop(lambda v: v.tensor_mul(t8b[:, :], t8b[:, :], revi_sb[:, :]))
        vop(lambda v: v.reduce_max(m2c[:, :], t8b[:, :], axis=AX.X))
        vop(lambda v: v.tensor_scalar(m2c[:, :], m2c[:, :], -1.0, 7.0, OP.mult, OP.add))
        vop(lambda v: v.tensor_scalar(t8b[:, :], iota_sb[:, :], m2c[:, :], None, OP.is_equal))
        vop(lambda v, sl=sl: v.tensor_scalar(w8_tm[:, sl], t8a[:, :], w1c[:, :], None, OP.mult))
        vop(lambda v, sl=sl: v.scalar_tensor_tensor(w8_tm[:, sl], t8b[:, :], w2c[:, :],
                                                    w8_tm[:, sl], OP.mult, OP.add))
        s.add("tensor", (lambda e, j=j, sl=sl: e.transpose(tps[:, :], w8_tm[:, sl], eye_sb[:, :])),
              waits=[("V", s.v("V"))], inc="PE")
        s.add("scalar", (lambda e, j=j: e.copy(w8e[:, j * 128:(j + 1) * 128], tps[:, :])),
              waits=[("PE", s.v("PE"))], inc="S")

    up_bf = [al.sb([128, 4096], BF16) for _ in range(KD)]
    dn_bf = [al.sb([128, D], BF16) for _ in range(16)]
    wstage = al.sb([128, 2048], F32)
    h_t = [_pool.ap().bitcast(BF16)[:, i * 512:(i + 1) * 512] for i in range(16)]
    wstage_g = _pool[:, 2048:4096]
    ug_st = al.sb([128, 512], BF16)
    gg_st = al.sb([128, 512], BF16)
    d_st = al.sb([128, 512], F32)
    acc_ch = x3_ch
    psg = [al.ps([128, 512], F32), al.ps([128, 512], F32)]
    se_g = [0, 0]
    gg_i = [0]

    for e in range(N_EXP):
        for c1, w1 in t_chunks(TPC):
            s.add("tensor", (lambda t, e=e, c1=c1, w1=w1:
                             t.matmul(bc_ps[:, :w1], selg[e][:, :],
                                      w8e[0:N_EXP, c1:c1 + w1], start=True, stop=True)),
                  waits=[("S", s.v("S")), ("V", s.v("V"))], inc="PE")
            s.add("scalar", (lambda e2, c1=c1, w1=w1: e2.copy(we_bc[:, c1:c1 + w1], bc_ps[:, :w1])),
                  waits=[("PE", s.v("PE"))], inc="S")
        parts = [("up", k, hh) for k in range(KD) for hh in range(2)] + \
                [("dn", k, None) for k in range(16)]
        qname = ["sync", "gpsimd"]
        qsem = ["LD", "GD"]
        qstage = [wstage, wstage_g]
        last_cast_v = [s.v("V"), s.v("V")]
        pe_gate = s.v("PE")
        for i, (kind, k, hh) in enumerate(parts):
            q = i % 2
            war = [("V", last_cast_v[q])]
            if i < 2:
                war += [("PE", pe_gate)]
            if kind == "up":
                s.add(qname[q], (lambda e2, e=e, k=k, hh=hh, st=qstage[q]:
                                 e2.dma_start(out=st[:, :], in_=wup_hbm[e, k * 128:(k + 1) * 128,
                                                                        hh * 2048:(hh + 1) * 2048])),
                      waits=war, inc=qsem[q], inc_amt=16)
                s.add("vector", (lambda v, k=k, hh=hh, st=qstage[q]:
                                 v.tensor_copy(up_bf[k][:, hh * 2048:(hh + 1) * 2048], st[:, :])),
                      waits=[(qsem[q], s.v(qsem[q]))], inc="V")
            else:
                s.add(qname[q], (lambda e2, e=e, k=k, st=qstage[q]:
                                 e2.dma_start(out=st[:, 0:D], in_=wdn_hbm[e, k * 128:(k + 1) * 128, :])),
                      waits=war, inc=qsem[q], inc_amt=16)
                s.add("vector", (lambda v, k=k, st=qstage[q]:
                                 v.tensor_copy(dn_bf[k][:, :], st[:, 0:D])),
                      waits=[(qsem[q], s.v(qsem[q]))], inc="V")
            last_cast_v[q] = s.v("V")
        w_ready_v = s.v("V")
        for ci, (c0, w) in enumerate(t_chunks(TPC)):
            for k in range(KD):
                s.add("sync", (lambda e2, k=k, c0=c0, w=w:
                               e2.dma_start(out=acc_ch[k][:, :w],
                                            in_=acc_hbm[k * 128:(k + 1) * 128, c0:c0 + w])),
                      waits=[("ST", s.v("ST")), ("V", s.v("V"))] if k == 0 else [],
                      inc="LD", inc_amt=16)
            ld_acc = s.v("LD")
            for m in range(16):
                paru = gg_i[0] % 2
                gg_i[0] += 1
                for k in range(KD):
                    s.add("tensor", (lambda t, m=m, k=k, c0=c0, w=w, paru=paru:
                                     t.matmul(psg[paru][:, :w], up_bf[k][:, m * 128:(m + 1) * 128],
                                              xn3[k][:, c0:c0 + w], start=(k == 0),
                                              stop=(k == KD - 1))),
                          waits=([("S", se_g[paru])] +
                                 ([("V", w_ready_v), ("S", s.v("S"))] if (m == 0 and ci == 0) else []))
                                if k == 0 else [],
                          inc="PE" if k == KD - 1 else None)
                s.add("scalar", (lambda e2, w=w, paru=paru: e2.activation(ug_st[:, :w], psg[paru][:, :w],
                                                               AF.Identity)),
                      waits=[("PE", s.v("PE")), ("V", s.v("V"))], inc="S")
                se_g[paru] = s.v("S")
                parg = gg_i[0] % 2
                gg_i[0] += 1
                for k in range(KD):
                    s.add("tensor", (lambda t, m=m, k=k, c0=c0, w=w, parg=parg:
                                     t.matmul(psg[parg][:, :w], up_bf[k][:, (16 + m) * 128:(17 + m) * 128],
                                              xn3[k][:, c0:c0 + w], start=(k == 0),
                                              stop=(k == KD - 1))),
                          waits=[("S", se_g[parg])] if k == 0 else [],
                          inc="PE" if k == KD - 1 else None)
                s.add("scalar", (lambda e2, w=w, parg=parg: e2.activation(gg_st[:, :w], psg[parg][:, :w],
                                                               AF.Silu)),
                      waits=[("PE", s.v("PE"))], inc="S")
                se_g[parg] = s.v("S")
                s.add("vector", (lambda v, m=m, w=w:
                                 v.tensor_mul(h_t[m][:, :w], ug_st[:, :w], gg_st[:, :w])),
                      waits=[("S", s.v("S"))], inc="V")
                s.add("scalar", lambda e2: e2.nop(), waits=[("V", s.v("V"))], inc="S")
            for md in range(KD):
                pard = gg_i[0] % 2
                gg_i[0] += 1
                for k in range(16):
                    s.add("tensor", (lambda t, md=md, k=k, w=w, pard=pard:
                                     t.matmul(psg[pard][:, :w], dn_bf[k][:, md * 128:(md + 1) * 128],
                                              h_t[k][:, :w], start=(k == 0), stop=(k == 15))),
                          waits=[("S", se_g[pard]), ("V", s.v("V"))] if k == 0 else [],
                          inc="PE" if k == 15 else None)
                s.add("scalar", (lambda e2, w=w, pard=pard: e2.activation(d_st[:, :w], psg[pard][:, :w],
                                                               AF.Identity)),
                      waits=[("PE", s.v("PE"))], inc="S")
                se_g[pard] = s.v("S")
                s.add("vector", (lambda v, md=md, c0=c0, w=w:
                                 v.scalar_tensor_tensor(d_st[:, :w], d_st[:, :w], 1.0,
                                                        d_st[:, :w], OP.mult, OP.add)
                                 if False else
                                 v.tensor_mul(d_st[:, :w], d_st[:, :w], we_bc[:, c0:c0 + w])),
                      waits=[("S", s.v("S")), ("LD", ld_acc)], inc="V")
                s.add("vector", (lambda v, md=md, w=w:
                                 v.tensor_add(acc_ch[md][:, :w], acc_ch[md][:, :w], d_st[:, :w])),
                      inc="V")
                s.add("scalar", lambda e2: e2.nop(), waits=[("V", s.v("V"))], inc="S")
            dst = acc_hbm if e < N_EXP - 1 else out_hbm
            for k in range(KD):
                s.add("gpsimd", (lambda e2, k=k, c0=c0, w=w, dst=dst:
                                 e2.dma_start(out=dst[k * 128:(k + 1) * 128, c0:c0 + w],
                                              in_=acc_ch[k][:, :w])),
                      waits=[("V", s.v("V"))], inc="ST", inc_amt=16)

    s.build(nc, [("ST", s.v("ST"))])
    return nc



def _win_idx(t0):
    return (np.arange(-HALO, TPC) + t0) % T


_ONES_COL = np.ones((128, 1), np.float32)
_ONES_ROW = np.ones((1, 128), np.float32)


def _mk_selbc():
    sel = np.zeros((32, 96, 128), np.float32)
    for n in range(16):
        sel[n, 64 + n, :] = 1.0
        sel[16 + n, 80 + n, :] = 1.0
    return sel.reshape(32 * 96, 128)


def _mk_selg():
    sel = np.zeros((N_EXP, N_EXP, 128), np.float32)
    for e in range(N_EXP):
        sel[e, e, :] = 1.0
    return sel.reshape(N_EXP * N_EXP, 128)


_SELBC = _mk_selbc()
_SELG = _mk_selg()


def _run(build_fn, in_maps):
    nc = build_fn()
    return run_bass_kernel_spmd(nc, in_maps, core_ids=list(range(N_CORES)))


def kernel(_debug=None, **inp):
    inp = {k: np.asarray(v) for k, v in inp.items()}
    x = inp["L0_attn_x"]

    xw = []
    for c in range(N_CORES):
        b, q = c // 4, c % 4
        idx = _win_idx(q * TPC)
        xw.append(np.ascontiguousarray(x[b, idx, :].T))

    resA = _run(build_launchA, [{
        "x_win": xw[c],
        "in_w": inp["L0_attn_in_w"],
        "in_b": inp["L0_attn_in_b"],
        "w1": inp["L0_attn_mod_w1"],
        "b1": inp["L0_attn_mod_b1"],
        "ones": _ONES_COL, "onesr": _ONES_ROW,
    } for c in range(N_CORES)])

    resB1 = _run(build_launchB1, [{
        "x_proj": resA.results[c]["x_proj"],
        "g": resA.results[c]["g"],
        "w2": inp["L0_attn_mod_w2"],
        "b2": inp["L0_attn_mod_b2"],
    } for c in range(N_CORES)])

    resB2 = _run(build_launchB2, [{
        "m": resB1.results[c]["m"],
        "ker": inp["L0_attn_kernel"],
        "out_w": inp["L0_attn_out_w"],
        "out_b": inp["L0_attn_out_b"],
        "x_win": xw[c],
        "mu_r": inp["L0_ffn_mu_r"],
        "mu_k": inp["L0_ffn_mu_k"],
        "first_mask": np.full((128, 1), 0.0 if c % 4 == 0 else 1.0, np.float32),
        "eye128": np.eye(128, dtype=np.float32),
        "ones": _ONES_COL, "onesr": _ONES_ROW,
    } for c in range(N_CORES)])

    WrT = np.ascontiguousarray(inp["L0_ffn_W_r"].T)
    WkT = np.ascontiguousarray(inp["L0_ffn_W_k"].T)
    WvT = np.ascontiguousarray(inp["L0_ffn_W_v"].T)
    resC = _run(build_launchC, [{
        "xr": resB2.results[c]["xr"],
        "xk": resB2.results[c]["xk"],
        "x1": resB2.results[c]["x1"],
        "WrT": WrT, "WkT": WkT, "WvT": WvT,
        "ones": _ONES_COL, "onesr": _ONES_ROW,
    } for c in range(N_CORES)])

    def mamba_layer(l, v_shards_bf, v_shards_res, vin_f32):
        in_wT = inp["L1_attn_in_proj_w"][l].T
        xp_wT = inp["L1_attn_x_proj_w"][l].T
        dt_wT = inp["L1_attn_dt_proj_w"][l].T
        out_wT = np.ascontiguousarray(inp["L1_attn_out_proj_w"][l].T)
        v_full = [np.concatenate([v_shards_bf[4 * b + j] for j in range(4)], axis=1)
                  for b in range(2)]
        mapsD = []
        for c in range(N_CORES):
            b, q = c // 4, c % 4
            sl = q * DSL
            mapsD.append({
                "v_full": v_full[b],
                "in_wT_sl": np.ascontiguousarray(
                    np.concatenate([in_wT[:, sl:sl + DSL],
                                    in_wT[:, D_INNER + sl:D_INNER + sl + DSL]], axis=1)),
                "conv_w_sl": np.ascontiguousarray(inp["L1_attn_conv_w"][l][sl:sl + DSL, :]),
                "conv_b_sl": np.ascontiguousarray(inp["L1_attn_conv_b"][l][sl:sl + DSL]),
                "xp_wT_sl": np.ascontiguousarray(xp_wT[sl:sl + DSL, :]),
            })
        resD = _run(build_launchD, mapsD)

        mapsE = []
        for c in range(N_CORES):
            b, q = c // 4, c % 4
            sl = q * DSL
            mapsE.append({
                "u": resD.results[c]["u"],
                "z": resD.results[c]["z"],
                "xdbl_all": np.stack([resD.results[4 * b + j]["xdbl_p"] for j in range(4)]),
                "dt_wT_sl": np.ascontiguousarray(dt_wT[:, sl:sl + DSL]),
                "dt_b_sl": np.ascontiguousarray(inp["L1_attn_dt_proj_b"][l][sl:sl + DSL]),
                "A_log_sl": np.ascontiguousarray(inp["L1_attn_A_log"][l][sl:sl + DSL, :]),
                "Dskip_sl": np.ascontiguousarray(inp["L1_attn_Dskip"][l][sl:sl + DSL]),
                "onesr": _ONES_ROW,
                "selBC": _SELBC,
            })
        resE = _run(build_launchE, mapsE)

        y_full = [np.concatenate([resE.results[4 * b + j]["y"] for j in range(4)], axis=0)
                  for b in range(2)]
        mapsF = []
        for c in range(N_CORES):
            b, q = c // 4, c % 4
            mapsF.append({
                "y_b": np.ascontiguousarray(y_full[b][:, q * TPC:(q + 1) * TPC]),
                "out_wT": out_wT,
                "ln_w": np.ascontiguousarray(inp["L1_attn_ln_w"][l]),
                "ln_b": np.ascontiguousarray(inp["L1_attn_ln_b"][l]),
                "v_in": v_shards_res[c],
                "ones": _ONES_COL, "onesr": _ONES_ROW,
            })
        resF = _run(lambda: build_launchF(vin_f32), mapsF)
        if _debug is not None:
            _debug[f"D{l}"] = resD
            _debug[f"E{l}"] = resE
            _debug[f"F{l}"] = resF
        return ([resF.results[c]["v_f"] for c in range(N_CORES)],
                [resF.results[c]["v_b"] for c in range(N_CORES)])

    v0_shards = [resC.results[c]["v0"] for c in range(N_CORES)]
    v1f, v1b = mamba_layer(0, v0_shards, v0_shards, vin_f32=False)
    v2f, v2b = mamba_layer(1, v1b, v1f, vin_f32=True)

    w_upT = np.ascontiguousarray(np.transpose(inp["L1_ffn_w_up"], (0, 2, 1)))
    w_downT = np.ascontiguousarray(np.transpose(inp["L1_ffn_w_down"], (0, 2, 1)))
    routerT = np.ascontiguousarray(inp["L1_ffn_w_router"].T)
    eye128 = np.eye(128, dtype=np.float32)
    iota8 = np.tile(np.arange(8, dtype=np.float32), (128, 1))
    revi8 = np.tile(7.0 - np.arange(8, dtype=np.float32), (128, 1))
    resG = _run(build_launchG, [{
        "x2": resC.results[c]["x2"],
        "v2": v2f[c],
        "norm_scale": inp["L1_ffn_norm_scale"],
        "routerT": routerT,
        "w_upT": w_upT,
        "w_downT": w_downT,
        "eye128": eye128, "iota8": iota8, "revi8": revi8, "selG": _SELG,
        "ones": _ONES_COL, "onesr": _ONES_ROW,
    } for c in range(N_CORES)])

    if _debug is not None:
        _debug.update(A=resA, B1=resB1, B2=resB2, C=resC, G=resG)

    out = np.empty((B, T, D), np.float32)
    for c in range(N_CORES):
        b, q = c // 4, c % 4
        out[b, q * TPC:(q + 1) * TPC, :] = resG.results[c]["out_fm"].T
    return out


if __name__ == "__main__":
    pass
